# version 07
# baseline (speedup 1.0000x reference)
import numpy as np
import ml_dtypes
import concourse.bass as bass
import concourse.mybir as mybir
from concourse.bass_utils import run_bass_kernel_spmd

F32 = mybir.dt.float32
BF16 = mybir.dt.bfloat16
AF = mybir.ActivationFunctionType
ALU = mybir.AluOpType
AX = mybir.AxisListType

B, S, D = 2, 8192, 1024
NH, DH = 16, 64
NCORES = 8
EXP_BIAS = -8.0

ENGINES = ("tensor", "vector", "scalar", "gpsimd", "sync")


class _EngQ:

    def __init__(self, prog, name, sem):
        self.prog = prog
        self.name = name
        self.sem = sem
        self.cnt = 0
        self.q = []
        self._waited = {}

    def wait(self, tok):
        if tok is None:
            return
        sem, val = tok
        key = id(sem)
        if self._waited.get(key, -1) >= val:
            return
        self._waited[key] = val
        self.q.append(lambda e: e.wait_ge(sem, val))

    def op(self, fn, sig=False, inc=1):
        if sig:
            self.cnt += inc
            sem = self.sem
            self.q.append(lambda e: fn(e).then_inc(sem, inc))
            return (sem, self.cnt)
        self.q.append(fn)
        return None

    def dma(self, out, in_):
        self.cnt += 16
        sem = self.sem
        self.q.append(lambda e: e.dma_start(out=out, in_=in_).then_inc(sem, 16))
        return (sem, self.cnt)

    def dma_accum(self, out, in_, first):
        self.cnt += 16
        sem = self.sem
        op = ALU.bypass if first else ALU.add
        self.q.append(lambda e: e.dma_start(out=out, in_=in_, accum_op=op)
                      .then_inc(sem, 16))
        return (sem, self.cnt)

    def tok(self):
        return (self.sem, self.cnt) if self.cnt else None


class Prog:
    def __init__(self, nc, ctx, blk):
        self.nc = nc
        self.blk = blk
        self.e = {}
        for name in ENGINES:
            sem = ctx.enter_context(nc.semaphore(f"sem_{name}"))
            self.e[name] = _EngQ(self, name, sem)
        self.tensor = self.e["tensor"]
        self.vector = self.e["vector"]
        self.scalar = self.e["scalar"]
        self.gpsimd = self.e["gpsimd"]
        self.sync = self.e["sync"]

    def flush(self):
        for name in ENGINES:
            q = self.e[name].q
            if not q:
                continue
            def runner(eng, _q=list(q)):
                for fn in _q:
                    fn(eng)
            getattr(self.blk, name)(runner)
            self.e[name].q = []



def build_launch_A():
    from contextlib import ExitStack
    nc = bass.Bass("TRN2", target_bir_lowering=False, debug=False,
                   detect_race_conditions=False)
    xbT = nc.dram_tensor("xbT", [D, S], F32, kind="ExternalInput").ap()
    wqT = nc.dram_tensor("wqT", [D, 256], F32, kind="ExternalInput").ap()
    wkT = nc.dram_tensor("wkT", [D, DH], F32, kind="ExternalInput").ap()
    wvT = nc.dram_tensor("wvT", [D, DH], F32, kind="ExternalInput").ap()
    attn = nc.dram_tensor("attn", [256, S], F32, kind="ExternalOutput").ap()

    NT = S // 512
    NKC = S // 128

    with ExitStack() as ctx:
        en = ctx.enter_context
        wq_sb = en(nc.sbuf_tensor("wq_sb", [128, 8 * 256], BF16))
        wk_sb = en(nc.sbuf_tensor("wk_sb", [128, 8 * DH], BF16))
        wv_sb = en(nc.sbuf_tensor("wv_sb", [128, 8 * DH], BF16))
        x_sb = [en(nc.sbuf_tensor(f"x_sb{i}", [128, 8 * 512], BF16)) for i in range(2)]
        qh_sb = [en(nc.sbuf_tensor(f"qh_sb{i}", [64, S], BF16)) for i in range(4)]
        k_sb = en(nc.sbuf_tensor("k_sb", [64, S], BF16))
        v_sb = en(nc.sbuf_tensor("v_sb", [128, NKC * 65], BF16))
        expb = [en(nc.sbuf_tensor(f"expb{i}", [128, 16 * 1024], BF16)) for i in range(2)]
        bc_sb = [en(nc.sbuf_tensor(f"bc_sb{i}", [64, 512], F32)) for i in range(2)]
        rd_sb = [en(nc.sbuf_tensor(f"rd_sb{i}", [1, 512], F32)) for i in range(2)]
        out_sb = [en(nc.sbuf_tensor(f"out_sb{i}", [64, 1024], F32)) for i in range(2)]
        ones_sb = en(nc.sbuf_tensor("ones_sb", [1, 64], F32))
        ebias_sb = en(nc.sbuf_tensor("ebias_sb", [128, 1], F32))
        zpad_sb = en(nc.sbuf_tensor("zpad_sbA", [128, 512], BF16))

        stbig = en(nc.psum_tensor("stbig", [128, 2048], F32))
        stp = [stbig[:, 0:1024], stbig[:, 1024:2048]]
        pvp = [en(nc.psum_tensor(f"pvp{i}", [128, 512], F32)) for i in range(2)]
        bcp = [en(nc.psum_tensor(f"bcp{i}", [128, 512], F32)) for i in range(2)]

        blk = en(nc.Block())
        p = Prog(nc, ctx, blk)
        T, V, A, G, SY = p.tensor, p.vector, p.scalar, p.gpsimd, p.sync

        tw = []
        tw.append(G.dma(wq_sb[:].rearrange("p (dc o) -> p dc o", o=256),
                        wqT.rearrange("(dc p) o -> p dc o", p=128)))
        tw.append(G.dma(wk_sb[:].rearrange("p (dc o) -> p dc o", o=DH),
                        wkT.rearrange("(dc p) o -> p dc o", p=128)))
        tw.append(G.dma(wv_sb[:].rearrange("p (dc o) -> p dc o", o=DH),
                        wvT.rearrange("(dc p) o -> p dc o", p=128)))
        tok_w = tw[-1]
        V.op(lambda e: e.memset(v_sb[:], 0.0))
        V.op(lambda e: e.memset(
            v_sb[:].rearrange("p (kc w) -> p kc w", w=65)[:, :, 64:65], 1.0))
        V.op(lambda e: e.memset(ebias_sb[:], EXP_BIAS))
        V.op(lambda e: e.memset(zpad_sb[:], 0.0))
        tok_ones = V.op(lambda e: e.memset(ones_sb[:], 1.0), sig=True)

        evac_tok = {}
        for tc in range(NT):
            xb = x_sb[tc % 2]
            if tc >= 2:
                G.wait(evac_tok.get(("x", tc % 2)))
            tok_x = G.dma(
                xb[:].rearrange("p (dc t) -> p dc t", t=512),
                xbT.rearrange("(dc p) t -> p dc t", p=128)[:, :, tc * 512:(tc + 1) * 512])
            T.wait(tok_x)
            T.wait(tok_w)
            T.wait(tok_ones)
            for oc in range(2):
                qv = stp[0][:, oc * 512:(oc + 1) * 512]
                T.wait(evac_tok.get(oc))
                def qmm(e, oc=oc, xb=xb, qv=qv):
                    for dc in range(8):
                        inst = e.matmul(qv,
                                        wq_sb[:, dc * 256 + oc * 128:dc * 256 + (oc + 1) * 128],
                                        xb[:, dc * 512:(dc + 1) * 512],
                                        start=(dc == 0), stop=(dc == 7))
                    return inst
                tq = T.op(qmm, sig=True)
                V.wait(tq)
                V.op(lambda e, oc=oc, tc=tc, qv=qv: e.tensor_copy(
                    qh_sb[2 * oc][:, tc * 512:(tc + 1) * 512], qv[0:64, :]))
                tev = V.op(lambda e, oc=oc, tc=tc, qv=qv: e.tensor_copy(
                    qh_sb[2 * oc + 1][:, tc * 512:(tc + 1) * 512], qv[64:128, :]),
                    sig=True)
                evac_tok[oc] = tev
            kv = stp[1][:, 0:512]
            T.wait(evac_tok.get(2))
            def kmm(e, xb=xb, kv=kv):
                for dc in range(8):
                    inst = e.matmul(kv[0:64, :],
                                    wk_sb[:, dc * DH:(dc + 1) * DH],
                                    xb[:, dc * 512:(dc + 1) * 512],
                                    start=(dc == 0), stop=(dc == 7))
                return inst
            tk = T.op(kmm, sig=True)
            V.wait(tk)
            tev = V.op(lambda e, tc=tc, kv=kv: e.tensor_copy(
                k_sb[:, tc * 512:(tc + 1) * 512], kv[0:64, :]), sig=True)
            evac_tok[2] = tev
            vv_ps = stp[1][:, 512:1024]
            for ts in range(4):
                kt = tc * 4 + ts
                T.wait(evac_tok.get(3))
                def vmm(e, xb=xb, ts=ts, vv_ps=vv_ps):
                    for dc in range(8):
                        e.matmul(vv_ps[:, 0:DH],
                                 xb[:, dc * 512 + ts * 128:dc * 512 + (ts + 1) * 128],
                                 wv_sb[:, dc * DH:(dc + 1) * DH],
                                 start=(dc == 0), stop=False)
                    return e.matmul(vv_ps[:, 0:512],
                                    xb[:, ts * 128:(ts + 1) * 128],
                                    zpad_sb[:, 0:512], start=False, stop=True)
                tv = T.op(vmm, sig=True)
                V.wait(tv)
                tev = V.op(lambda e, kt=kt, vv_ps=vv_ps: e.tensor_copy(
                    v_sb[:, kt * 65:kt * 65 + 64], vv_ps[:, 0:DH]), sig=True)
                evac_tok[3] = tev
            evac_tok[("x", tc % 2)] = tv

        tok_p1 = evac_tok[3]
        T.wait(tok_p1)
        NQ2 = S // 1024
        KBLK = 16
        jobs = [(h, qc) for h in range(4) for qc in range(NQ2)]
        NB = len(jobs) * 4
        last_pv_use = {}
        last_bc_use = {}
        last_rd_use = {}
        store_tok = {}
        exp_tok = {}
        exp_blk = {}
        pv_blk = {}
        st_tok = {}

        def tail(j):
            h, qc = jobs[j]
            tpv = pv_blk[4 * j + 3]
            V.wait(tpv)
            tnorm = None
            for half, pvb, bcb, key in ((0, pvp[0], bcp[0], "a"),
                                        (1, pvp[1], bcp[1], "b")):
                V.wait(last_rd_use.get(key))
                trd = V.op(lambda e, half=half, pvb=pvb: e.reciprocal(
                    rd_sb[half][:, :], pvb[64:65, :]), sig=True)
                T.wait(trd)
                T.wait(last_bc_use.get(key))
                tbc = T.op(lambda e, bcb=bcb, half=half: e.matmul(
                    bcb[0:64, :], ones_sb[0:1, 0:64], rd_sb[half][:, :],
                    start=True, stop=True), sig=True)
                last_rd_use[key] = tbc
                V.wait(tbc)
                tcp = V.op(lambda e, bcb=bcb, half=half: e.tensor_copy(
                    bc_sb[half][:, :], bcb[0:64, :]), sig=True)
                last_bc_use[key] = tcp
                V.wait(store_tok.get(j % 2))
                tnorm = V.op(lambda e, half=half, pvb=pvb, j=j: e.tensor_mul(
                    out_sb[j % 2][:, half * 512:(half + 1) * 512],
                    pvb[0:64, :], bc_sb[half][:, :]), sig=True)
                last_pv_use[key] = tnorm
            SY.wait(tnorm)
            store_tok[j % 2] = SY.dma(
                attn[64 * h:64 * (h + 1), qc * 1024:(qc + 1) * 1024],
                out_sb[j % 2][:, :])

        for Bl in range(NB + 1):
            if Bl >= 1:
                Bp = Bl - 1
                T.wait(exp_blk[Bp])
                for kk in range(KBLK):
                    kcp = (Bp % 4) * KBLK + kk
                    if kcp == 0:
                        T.wait(last_pv_use.get("a"))
                        T.wait(last_pv_use.get("b"))
                    def pvf(e, Bp=Bp, kk=kk, kcp=kcp):
                        e.matmul(pvp[0][0:65, :],
                                 v_sb[:, kcp * 65:(kcp + 1) * 65],
                                 expb[Bp % 2][:, kk * 1024:kk * 1024 + 512],
                                 start=(kcp == 0), stop=(kcp == NKC - 1))
                        return e.matmul(pvp[1][0:65, :],
                                        v_sb[:, kcp * 65:(kcp + 1) * 65],
                                        expb[Bp % 2][:, kk * 1024 + 512:(kk + 1) * 1024],
                                        start=(kcp == 0), stop=(kcp == NKC - 1))
                    tpv = T.op(pvf, sig=(kk == KBLK - 1))
                    if kk == KBLK - 1:
                        pv_blk[Bp] = tpv
            if Bl < NB:
                j = Bl // 4
                h, qc = jobs[j]
                qsb = qh_sb[h]
                for kk in range(KBLK):
                    kc = (Bl % 4) * KBLK + kk
                    g = Bl * KBLK + kk
                    stt = stp[g % 2]
                    T.wait(exp_tok.get(g - 2))
                    def stf(e, stt=stt, kc=kc, qsb=qsb, qc=qc):
                        e.matmul(stt[:, 0:512],
                                 k_sb[:, kc * 128:(kc + 1) * 128],
                                 qsb[:, qc * 1024:qc * 1024 + 512],
                                 start=True, stop=True)
                        return e.matmul(stt[:, 512:1024],
                                        k_sb[:, kc * 128:(kc + 1) * 128],
                                        qsb[:, qc * 1024 + 512:(qc + 1) * 1024],
                                        start=True, stop=True)
                    st_tok[g] = T.op(stf, sig=True)
                    if kk % 2 == 1:
                        A.wait(st_tok[g])
                        A.wait(pv_blk.get(Bl - 2))
                        tpair = A.op(lambda e, Bl=Bl, kk=kk: e.activation(
                            expb[Bl % 2][:, (kk - 1) * 1024:(kk + 1) * 1024],
                            stbig[:, 0:2048], AF.Exp,
                            bias=ebias_sb[:, 0:1], scale=0.125), sig=True)
                        exp_tok[g] = tpair
                        exp_tok[g - 1] = tpair
                    if kk == KBLK - 1:
                        exp_blk[Bl] = exp_tok[g]
            if Bl >= 1 and Bl % 4 == 0:
                tail(Bl // 4 - 1)
        tail(len(jobs) - 1)
        SY.wait(store_tok.get(0))
        SY.wait(store_tok.get(1))
        p.flush()
    return nc




TB = 2048


def _emit_ln_stats_norm(p, nc, src_sb, dst_bf, sc, pbank, T_tok):
    T, V, A = p.tensor, p.vector, p.scalar
    sum_ps, ssq_ps, bc1_ps, bc2_ps = pbank
    ntc = T_tok // 512
    prev_mm = None
    tlast = None
    for tc in range(ntc):
        for dc in range(8):
            col = dc * T_tok + tc * 512
            i = dc % 2
            V.wait(prev_mm if dc >= 2 or tc > 0 else None)
            V.op(lambda e, col=col, i=i: e.tensor_copy(
                sc["xb1"][i][:, 0:512], src_sb[:, col:col + 512]))
            tsq = V.op(lambda e, col=col, i=i: e.tensor_mul(
                sc["xb2"][i][:, 0:512], src_sb[:, col:col + 512],
                src_sb[:, col:col + 512]), sig=True)
            T.wait(tsq)
            T.op(lambda e, dc=dc, i=i: e.matmul(
                sum_ps[0:1, :], sc["onescol_bf"][:, 0:1], sc["xb1"][i][:, 0:512],
                start=(dc == 0), stop=(dc == 7)))
            prev_mm = T.op(lambda e, dc=dc, i=i: e.matmul(
                ssq_ps[0:1, :], sc["onescol_bf"][:, 0:1], sc["xb2"][i][:, 0:512],
                start=(dc == 0), stop=(dc == 7)), sig=True)
        ssum_tok = prev_mm
        V.wait(ssum_tok)
        V.op(lambda e: e.tensor_scalar_mul(sc["mean"][:, :], sum_ps[0:1, :], 1.0 / D))
        V.op(lambda e: e.tensor_scalar_mul(sc["msq"][:, :], ssq_ps[0:1, :], 1.0 / D))
        V.op(lambda e: e.tensor_mul(sc["m2"][:, :], sc["mean"][:, :], sc["mean"][:, :]))
        tvar = V.op(lambda e: e.tensor_sub(sc["msq"][:, :], sc["msq"][:, :],
                                           sc["m2"][:, :]), sig=True)
        A.wait(tvar)
        A.op(lambda e: e.activation(sc["m2"][:, :], sc["msq"][:, :], AF.Ln,
                                    bias=sc["eps"][0:1, 0:1]))
        trs = A.op(lambda e: e.activation(sc["msq"][:, :], sc["m2"][:, :], AF.Exp,
                                          scale=-0.5), sig=True)
        V.wait(trs)
        tmr = V.op(lambda e: e.tensor_mul(sc["m2"][:, :], sc["mean"][:, :],
                                          sc["msq"][:, :]), sig=True)
        T.wait(tmr)
        tb1 = T.op(lambda e: e.matmul(bc1_ps[:, :], sc["onesrow"][0:1, 0:128],
                                      sc["msq"][:, :], start=True, stop=True),
                   sig=True)
        tb2 = T.op(lambda e: e.matmul(bc2_ps[:, :], sc["onesrow"][0:1, 0:128],
                                      sc["m2"][:, :], start=True, stop=True),
                   sig=True)
        V.wait(tb1)
        V.wait(tb2)
        tlast = None
        for dc in range(8):
            col = dc * T_tok + tc * 512
            V.op(lambda e, col=col: e.tensor_mul(
                src_sb[:, col:col + 512], src_sb[:, col:col + 512], bc1_ps[:, :]))
            V.op(lambda e, col=col: e.tensor_sub(
                src_sb[:, col:col + 512], src_sb[:, col:col + 512], bc2_ps[:, :]))
            tlast = V.op(lambda e, col=col: e.tensor_copy(
                dst_bf[:, col:col + 512], src_sb[:, col:col + 512]),
                sig=(dc == 7))
        T.wait(tlast)
    return tlast


def _vbar(p, sc):
    V = p.vector
    t = V.op(lambda e: e.tensor_add(sc["bar"][:, :], sc["bar"][:, :],
                                    sc["bar"][:, :]), sig=True)
    return t


def _emit_router(p, nc, xbf, sc, logit_ps, gatew_sb, T_tok, renorm,
                 extra_bias_mm=None):
    T, V, A = p.tensor, p.vector, p.scalar
    ntile = T_tok // 128
    tok_g = None
    for tt in range(ntile):
        has_bias = extra_bias_mm is not None
        def lmm(e, tt=tt, has_bias=has_bias):
            for dc in range(8):
                inst = e.matmul(logit_ps[:, 0:8],
                                xbf[:, dc * T_tok + tt * 128: dc * T_tok + (tt + 1) * 128],
                                gatew_sb[:, dc * 8:(dc + 1) * 8],
                                start=(dc == 0), stop=False)
            return inst
        T.op(lmm)
        if has_bias:
            T.op(lambda e: extra_bias_mm(e, logit_ps))
        tl = T.op(lambda e, tt=tt: e.matmul(
            logit_ps[:, 0:512],
            xbf[:, tt * 128:(tt + 1) * 128],
            sc["zpad"][:, 0:512], start=False, stop=True), sig=True)
        V.wait(tl)
        tnm = V.op(lambda e: e.reduce_max(sc["nmax"][:, :], logit_ps[:, 0:8],
                                          axis=AX.X, negate=True), sig=True)
        A.wait(tnm)
        tex = A.op(lambda e: e.activation(sc["el"][:, :], logit_ps[:, 0:8], AF.Exp,
                                          bias=sc["nmax"][:, 0:1]), sig=True)
        V.wait(tex)
        trsum = V.op(lambda e: e.reduce_sum(sc["rs"][:, :], sc["el"][:, :],
                                            axis=AX.X), sig=True)
        A.wait(trsum)
        A.op(lambda e: e.activation(sc["rr"][:, :], sc["rs"][:, :], AF.Ln))
        trr = A.op(lambda e: e.activation(sc["rr"][:, :], sc["rr"][:, :], AF.Exp,
                                          scale=-1.0), sig=True)
        V.wait(trr)
        tsc = V.op(lambda e: e.tensor_scalar_mul(sc["scores"][:, :], sc["el"][:, :],
                                                 sc["rr"][:, 0:1]), sig=True)
        V.wait(tsc)
        tmx = V.op(lambda e: e.max(out=sc["t8"][:, :], in_=sc["scores"][:, :]),
                   sig=True)
        V.wait(tmx)
        tse = V.op(lambda e: e.tensor_single_scalar(sc["sel"][:, :], sc["scores"][:, :],
                                                    sc["t8"][:, 1:2], ALU.is_ge),
                   sig=True)
        V.wait(tse)
        tg = V.op(lambda e: e.tensor_mul(sc["g"][:, :], sc["scores"][:, :],
                                         sc["sel"][:, :]), sig=True)
        if renorm:
            tden = V.op(lambda e: e.tensor_add(sc["den"][:, :], sc["t8"][:, 0:1],
                                               sc["t8"][:, 1:2]), sig=True)
            A.wait(tden)
            A.op(lambda e: e.activation(sc["rden"][:, :], sc["den"][:, :], AF.Ln))
            trd = A.op(lambda e: e.activation(sc["rden"][:, :], sc["rden"][:, :],
                                              AF.Exp, scale=-1.0), sig=True)
            V.wait(trd)
            V.wait(tg)
            tg = V.op(lambda e: e.tensor_scalar_mul(sc["g"][:, :], sc["g"][:, :],
                                                    sc["rden"][:, 0:1]), sig=True)
        V.wait(tg)
        tgc = V.op(lambda e: e.tensor_copy(sc["g_bf"][:, :], sc["g"][:, :]),
                   sig=True)
        V.wait(tgc)
        tsc2 = V.op(lambda e: e.tensor_copy(sc["sel_bf"][:, :], sc["sel"][:, :]),
                    sig=True)
        T.wait(tsc2)
        ttr = T.op(lambda e: e.matmul(sc["gt_ps"][0:8, 0:512], sc["g_bf"][:, :],
                                      sc["identb"][:, 0:512], start=True, stop=True),
                   sig=True)
        ttr2 = T.op(lambda e: e.matmul(sc["st_ps"][0:8, 0:512], sc["sel_bf"][:, :],
                                       sc["identb"][:, 0:512], start=True, stop=True),
                    sig=True)
        V.wait(ttr2)
        V.op(lambda e, tt=tt: e.tensor_copy(
            sc["g_fm"][:, tt * 128:(tt + 1) * 128], sc["gt_ps"][0:8, 0:128]))
        tok_g = V.op(lambda e, tt=tt: e.tensor_copy(
            sc["sel_fm"][:, tt * 128:(tt + 1) * 128], sc["st_ps"][0:8, 0:128]),
            sig=True)
        T.wait(tok_g)
    return tok_g


def build_launch_B():
    from contextlib import ExitStack
    nc = bass.Bass("TRN2", target_bir_lowering=False, debug=False,
                   detect_race_conditions=False)
    xT = nc.dram_tensor("xT", [D, TB], F32, kind="ExternalInput").ap()
    attnT = nc.dram_tensor("attnT", [D, TB], F32, kind="ExternalInput").ap()
    w1T = nc.dram_tensor("w1T", [8, D, D], F32, kind="ExternalInput").ap()
    w2T = nc.dram_tensor("w2T", [8, D, D], F32, kind="ExternalInput").ap()
    w3T = nc.dram_tensor("w3T", [8, D, D], F32, kind="ExternalInput").ap()
    sw1T = nc.dram_tensor("sw1T", [D, 2048], F32, kind="ExternalInput").ap()
    sw2T = nc.dram_tensor("sw2T", [D, 2048], F32, kind="ExternalInput").ap()
    sw3T = nc.dram_tensor("sw3T", [2048, D], F32, kind="ExternalInput").ap()
    gate_pk = nc.dram_tensor("gate_pk", [128, 64], F32, kind="ExternalInput").ap()
    bias_pk = nc.dram_tensor("bias_pk", [128, 192], F32, kind="ExternalInput").ap()
    ident_in = nc.dram_tensor("ident_in", [128, 512], F32, kind="ExternalInput").ap()
    oneh_in = nc.dram_tensor("oneh_in", [40, 8 * 128], F32, kind="ExternalInput").ap()
    x2out = nc.dram_tensor("x2out", [D, TB], F32, kind="ExternalOutput").ap()

    dram_x1 = nc.dram_tensor("dram_x1", [128, 8 * TB], F32, kind="Internal").ap()
    dram_y = nc.dram_tensor("dram_y", [128, 8 * TB], F32, kind="Internal").ap()
    dbg_x1 = nc.dram_tensor("dbg_x1", [128, 8 * TB], F32, kind="ExternalOutput").ap()
    dbg_y = nc.dram_tensor("dbg_y", [128, 8 * TB], F32, kind="ExternalOutput").ap()
    dbg_r2 = nc.dram_tensor("dbg_r2", [128, 8 * TB], F32, kind="ExternalOutput").ap()
    dbg_gs = nc.dram_tensor("dbg_gs", [40, TB], F32, kind="ExternalOutput").ap()
    dbg_h = nc.dram_tensor("dbg_h", [128, 8 * 512], F32, kind="ExternalOutput").ap()
    dbg_x3 = nc.dram_tensor("dbg_x3", [128, 8 * 512], F32, kind="ExternalOutput").ap()
    dbg_gb = nc.dram_tensor("dbg_gb", [128, 1024], F32, kind="ExternalOutput").ap()

    TH = 512

    with ExitStack() as ctx:
        en = ctx.enter_context
        res_sb = en(nc.sbuf_tensor("res_sb", [128, 8 * TB], F32))
        stage_sb = en(nc.sbuf_tensor("stage_sb", [128, 8 * 512], F32))
        xbf = en(nc.sbuf_tensor("xbf", [128, 8 * TB], BF16))
        h_sb = en(nc.sbuf_tensor("h_sb", [128, 8 * TH], BF16))
        x3_sb = en(nc.sbuf_tensor("x3_sb", [128, 8 * TH], BF16))
        gbc_sb = en(nc.sbuf_tensor("gbc_sb", [128, TH], F32))
        selbc_sb = en(nc.sbuf_tensor("selbc_sb", [128, TH], BF16))
        u_sb = [stage_sb[:, 0:512], stage_sb[:, 512:1024]]
        t_sb = [stage_sb[:, 1024:1536], stage_sb[:, 1536:2048]]
        sl_sb = [stage_sb[:, 2048:2560], stage_sb[:, 2560:3072]]
        w1_sb = [en(nc.sbuf_tensor(f"w1_sb{i}", [128, 8 * 1024], BF16)) for i in range(2)]
        w2_sb = en(nc.sbuf_tensor("w2_sb", [128, 8 * 1024], BF16))
        w3_sb = en(nc.sbuf_tensor("w3_sb", [128, 8 * 1024], BF16))
        swc = [{1: w1_sb[0][:, 0:4096], 2: w1_sb[0][:, 4096:8192],
                3: w1_sb[1][:, 0:4096]},
               {1: w1_sb[1][:, 4096:8192], 2: w2_sb[:, 0:4096],
                3: w2_sb[:, 4096:8192]}]
        hh_sb = w3_sb[:, 0:8192]
        gatew_sb = en(nc.sbuf_tensor("gatew_sb", [128, 64], BF16))
        bias_sb = en(nc.sbuf_tensor("bias_sb", [128, 192], F32))
        ident_sb = en(nc.sbuf_tensor("ident_sb", [128, 512], F32))
        oneh_sb = en(nc.sbuf_tensor("oneh_sb", [40, 8 * 128], BF16))
        gs_pack = en(nc.sbuf_tensor("gs_pack", [40, TB], BF16))
        g_fm = gs_pack[0:8, :]
        sel_fm = gs_pack[32:40, :]
        sc = {}
        sc["xb1"] = [h_sb[:, 0:512], h_sb[:, 512:1024]]
        sc["xb2"] = [h_sb[:, 1024:1536], h_sb[:, 1536:2048]]
        sc["mean"] = stage_sb[0:1, 3072:3584]
        sc["msq"] = stage_sb[0:1, 3584:4096]
        sc["m2"] = stage_sb[0:1, 2048:2560]
        for nm in ("nmax", "rs", "rr", "den", "rden"):
            sc[nm] = en(nc.sbuf_tensor(f"rt_{nm}", [128, 1], F32))
        for nm in ("el", "scores", "t8", "sel", "g"):
            sc[nm] = en(nc.sbuf_tensor(f"rt_{nm}", [128, 8], F32))
        sc["eps"] = en(nc.sbuf_tensor("eps_sb", [1, 1], F32))
        sc["zpad"] = en(nc.sbuf_tensor("zpad_sb", [128, 512], BF16))
        sc["g_bf"] = en(nc.sbuf_tensor("g_bf", [128, 8], BF16))
        sc["sel_bf"] = en(nc.sbuf_tensor("sel_bf", [128, 8], BF16))
        sc["identb"] = en(nc.sbuf_tensor("identb_sb", [128, 512], BF16))
        sc["onescol_bf"] = en(nc.sbuf_tensor("onescol_bf", [128, 1], BF16))
        sc["onesrow"] = en(nc.sbuf_tensor("onesrow", [1, 128], F32))
        sc["ident"] = ident_sb
        sc["g_fm"] = g_fm
        sc["sel_fm"] = sel_fm

        pb = [en(nc.psum_tensor(f"pb{i}", [128, 512], F32)) for i in range(8)]
        sc["gt_ps"] = pb[6]
        sc["st_ps"] = pb[7]

        blk = en(nc.Block())
        p = Prog(nc, ctx, blk)
        T, V, A, G, SY = p.tensor, p.vector, p.scalar, p.gpsimd, p.sync

        G.dma(gatew_sb[:], gate_pk)
        SY.dma(bias_sb[:], bias_pk)
        tok_id = SY.dma(ident_sb[:], ident_in)
        G.dma(sc["identb"][:], ident_in)
        tok_g0 = G.dma(oneh_sb[:], oneh_in)
        T.wait(tok_id)
        V.op(lambda e: e.memset(sc["eps"][:], 1e-5))
        V.op(lambda e: e.memset(sc["onescol_bf"][:], 1.0))
        V.op(lambda e: e.memset(sc["zpad"][:], 0.0))
        tok_k = V.op(lambda e: e.memset(sc["onesrow"][:], 1.0), sig=True)
        T.wait(tok_g0)
        A.wait(tok_g0)

        resr = res_sb[:].rearrange("p (dc t) -> p dc t", t=TB)
        tadd = None
        for tc in range(4):
            tx = SY.dma(resr[:, :, tc * 512:(tc + 1) * 512],
                        xT.rearrange("(dc p) t -> p dc t", p=128)[:, :, tc * 512:(tc + 1) * 512])
            if tadd is not None:
                SY.wait(tadd)
            ts = SY.dma(stage_sb[:].rearrange("p (dc t) -> p dc t", t=512),
                        attnT.rearrange("(dc p) t -> p dc t", p=128)[:, :, tc * 512:(tc + 1) * 512])
            V.wait(tx)
            V.wait(ts)
            for dc in range(8):
                tadd = V.op(lambda e, dc=dc, tc=tc: e.tensor_add(
                    res_sb[:, dc * TB + tc * 512:dc * TB + (tc + 1) * 512],
                    res_sb[:, dc * TB + tc * 512:dc * TB + (tc + 1) * 512],
                    stage_sb[:, dc * 512:(dc + 1) * 512]), sig=(dc == 7))
        T.wait(tok_k)
        T.wait(tadd)
        tok_ln1 = _emit_ln_stats_norm(p, nc, res_sb, xbf, sc,
                                      [pb[0], pb[1], pb[2], pb[3]], TB)

        SY.wait(tok_ln1)
        tok_spill = SY.dma(dram_x1, res_sb[:])
        SY.dma(dbg_x1, res_sb[:])

        T.wait(tok_ln1)
        tok_rt = _emit_router(p, nc, xbf, sc, pb[5], gatew_sb, TB, renorm=False)

        G.wait(tok_rt)
        G.dma(dbg_gs, gs_pack[:])

        tok_w1 = [None, None]
        tok_w1[0] = G.dma(w1_sb[0][:].rearrange("p (dc o) -> p dc o", o=1024),
                          w1T[0].rearrange("(dc p) o -> p dc o", p=128))
        tok_w2 = G.dma(w2_sb[:].rearrange("p (dc o) -> p dc o", o=1024),
                       w2T[0].rearrange("(dc p) o -> p dc o", p=128))
        tok_w3 = G.dma(w3_sb[:].rearrange("p (dc o) -> p dc o", o=1024),
                       w3T[0].rearrange("(dc p) o -> p dc o", p=128))
        pmask = {}
        wlast = {}
        ydma = []
        A.wait(tok_ln1)
        for e_i in range(8):
            w1b = w1_sb[e_i % 2]
            if e_i + 1 < 8:
                G.wait(wlast.get(("w1", (e_i + 1) % 2)))
                tok_w1[(e_i + 1) % 2] = G.dma(
                    w1_sb[(e_i + 1) % 2][:].rearrange("p (dc o) -> p dc o", o=1024),
                    w1T[e_i + 1].rearrange("(dc p) o -> p dc o", p=128))
            for th in range(4):
                T.wait(tok_rt)
                T.wait(pmask.get(4))
                tg1 = T.op(lambda e, e_i=e_i, th=th: e.matmul(
                    pb[4][:, :], oneh_sb[0:8, e_i * 128:(e_i + 1) * 128],
                    g_fm[:, th * TH:(th + 1) * TH], start=True, stop=True), sig=True)
                T.wait(pmask.get(5))
                tg2 = T.op(lambda e, e_i=e_i, th=th: e.matmul(
                    pb[5][:, :], oneh_sb[32:40, e_i * 128:(e_i + 1) * 128],
                    sel_fm[:, th * TH:(th + 1) * TH], start=True, stop=True), sig=True)
                V.wait(tg1)
                tgb = V.op(lambda e: e.tensor_copy(gbc_sb[:, 0:TH], pb[4][:, 0:TH]),
                           sig=True)
                V.wait(tg2)
                tsb = V.op(lambda e: e.tensor_copy(selbc_sb[:, 0:TH], pb[5][:, 0:TH]),
                           sig=True)
                pmask[4] = tgb
                pmask[5] = tsb
                for which, wsb, tok_wx in ((1, w1b, tok_w1[e_i % 2]),
                                           (3, w3_sb, tok_w3)):
                    for hc in range(8):
                        bank = hc % 2
                        T.wait(tok_wx)
                        T.wait(pmask.get(bank))
                        def pmm(e, wsb=wsb, hc=hc, th=th, bank=bank):
                            for dc in range(8):
                                inst = e.matmul(
                                    pb[bank][:, :],
                                    wsb[:, dc * 1024 + hc * 128:dc * 1024 + (hc + 1) * 128],
                                    xbf[:, dc * TB + th * TH:dc * TB + (th + 1) * TH],
                                    start=(dc == 0), stop=(dc == 7))
                            return inst
                        tp = T.op(pmm, sig=True)
                        V.wait(tp)
                        V.wait(tgb)
                        if which == 1:
                            V.wait(wlast.get(("u", bank)))
                            tu = V.op(lambda e, bank=bank: e.tensor_mul(
                                u_sb[bank][:, :], pb[bank][:, :], gbc_sb[:, 0:TH]),
                                sig=True)
                            A.wait(tu)
                            A.wait(wlast.get("h_read"))
                            boff = (e_i * 3 + 0) * 8 + hc
                            tsil = A.op(lambda e, bank=bank, hc=hc, boff=boff:
                                        e.activation(h_sb[:, hc * TH:(hc + 1) * TH],
                                                     u_sb[bank][:, :], AF.Silu,
                                                     bias=bias_sb[:, boff:boff + 1]),
                                        sig=True)
                            pmask[bank] = tu
                            wlast[("u", bank)] = tsil
                        else:
                            tx3 = V.op(lambda e, bank=bank, hc=hc: e.tensor_mul(
                                x3_sb[:, hc * TH:(hc + 1) * TH], pb[bank][:, :],
                                gbc_sb[:, 0:TH]), sig=True)
                            pmask[bank] = tx3
                    wlast[("w1", e_i % 2) if which == 1 else "w3"] = tp
                for dcc in range(8):
                    bank = 2 + dcc % 2
                    T.wait(tok_w2)
                    T.wait(wlast.get(("u", 0)))
                    T.wait(wlast.get(("u", 1)))
                    T.wait(pmask.get(bank))
                    def p2mm(e, dcc=dcc):
                        for hc in range(8):
                            inst = e.matmul(
                                pb[2 + dcc % 2][:, :],
                                w2_sb[:, hc * 1024 + dcc * 128:hc * 1024 + (dcc + 1) * 128],
                                h_sb[:, hc * TH:(hc + 1) * TH],
                                start=(hc == 0), stop=(hc == 7))
                        return inst
                    tp2 = T.op(p2mm, sig=True)
                    V.wait(tp2)
                    V.wait(wlast.get(("u", bank - 2)))
                    b2off = (e_i * 3 + 1) * 8 + dcc
                    tu2 = V.op(lambda e, bank=bank, b2off=b2off: e.scalar_tensor_tensor(
                        u_sb[bank - 2][:, :], pb[bank][:, :], bias_sb[:, b2off:b2off + 1],
                        selbc_sb[:, 0:TH], ALU.add, ALU.mult), sig=True)
                    pmask[bank] = tu2
                    wlast[("u", bank - 2)] = tu2
                    b3off = (e_i * 3 + 2) * 8 + dcc
                    V.wait(wlast.get(("tdma", bank - 2)))
                    tprod = V.op(lambda e, bank=bank, dcc=dcc, b3off=b3off:
                                 e.scalar_tensor_tensor(
                                     t_sb[bank - 2][:, :],
                                     x3_sb[:, dcc * TH:(dcc + 1) * TH],
                                     bias_sb[:, b3off:b3off + 1],
                                     u_sb[bank - 2][:, :], ALU.add, ALU.mult),
                                 sig=True)
                    G.wait(tprod)
                    G.wait(wlast.get(("yreg", th, dcc)))
                    tok_y = G.dma_accum(
                        dram_y[:, dcc * TB + th * TH:dcc * TB + (th + 1) * TH],
                        t_sb[bank - 2][:, :], first=(e_i == 0))
                    ydma.append(tok_y)
                    wlast[("tdma", bank - 2)] = tok_y
                    wlast[("yreg", th, dcc)] = tok_y
                    wlast["h_read"] = tp2
            if e_i + 1 < 8:
                G.wait(wlast.get("w3"))
                tok_w3 = G.dma(w3_sb[:].rearrange("p (dc o) -> p dc o", o=1024),
                               w3T[e_i + 1].rearrange("(dc p) o -> p dc o", p=128))
                G.wait(wlast.get("h_read"))
                tok_w2 = G.dma(w2_sb[:].rearrange("p (dc o) -> p dc o", o=1024),
                               w2T[e_i + 1].rearrange("(dc p) o -> p dc o", p=128))
        tok_ylast = ydma[-1]
        G.wait(tok_ylast)
        G.dma(dbg_y, dram_y)
        G.dma(dbg_h, h_sb[:])
        G.dma(dbg_x3, x3_sb[:])
        G.dma(dbg_gb[:, 0:512], gbc_sb[:])
        G.dma(dbg_gb[:, 512:1024], selbc_sb[:])

        SY.wait(tok_spill)
        tln = None
        for tc in range(4):
            tx = SY.dma(resr[:, :, tc * 512:(tc + 1) * 512],
                        dram_x1.rearrange("p (dc t) -> p dc t", t=TB)[:, :, tc * 512:(tc + 1) * 512])
            if tln is not None:
                G.wait(tln)
            G.wait(tok_ylast)
            ty = G.dma(stage_sb[:].rearrange("p (dc t) -> p dc t", t=512),
                       dram_y.rearrange("p (dc t) -> p dc t", t=TB)[:, :, tc * 512:(tc + 1) * 512])
            V.wait(tx)
            V.wait(ty)
            for dc in range(8):
                tln = V.op(lambda e, dc=dc, tc=tc: e.tensor_add(
                    res_sb[:, dc * TB + tc * 512:dc * TB + (tc + 1) * 512],
                    res_sb[:, dc * TB + tc * 512:dc * TB + (tc + 1) * 512],
                    stage_sb[:, dc * 512:(dc + 1) * 512]), sig=(dc == 7))
        G.wait(wlast.get(("w1", 0)))
        G.wait(wlast.get(("w1", 1)))
        G.wait(wlast.get("w3"))
        G.wait(wlast.get("h_read"))
        tok_sw = [None, None]
        def load_sw(i, ic):
            G.dma(swc[i][1][:].rearrange("p (dc t) -> p dc t", t=512),
                  sw1T.rearrange("(dc p) o -> p dc o", p=128)[:, :, ic * 512:(ic + 1) * 512])
            G.dma(swc[i][2][:].rearrange("p (dc t) -> p dc t", t=512),
                  sw2T.rearrange("(dc p) o -> p dc o", p=128)[:, :, ic * 512:(ic + 1) * 512])
            return G.dma(swc[i][3][:].rearrange("p (ic o) -> p ic o", o=1024),
                         sw3T.rearrange("(icc p) o -> p icc o", p=128)[:, ic * 4:(ic + 1) * 4, :])
        tok_sw[0] = load_sw(0, 0)
        swlast = {}
        tza = None
        for ic in range(4):
            i = ic % 2
            if ic + 1 < 4:
                G.wait(swlast.get((ic + 1) % 2))
                tok_sw[(ic + 1) % 2] = load_sw((ic + 1) % 2, ic + 1)
            for isc in range(4):
                for tc in range(4):
                    b1k, b2k = isc % 2, 2 + isc % 2
                    T.wait(tok_sw[i])
                    T.wait(pmask.get(("sh", b1k)))
                    def h1mm(e, i=i, isc=isc, tc=tc, b1k=b1k):
                        for dc in range(8):
                            inst = e.matmul(
                                pb[b1k][:, :],
                                swc[i][1][:, dc * 512 + isc * 128:dc * 512 + (isc + 1) * 128],
                                xbf[:, dc * TB + tc * 512:dc * TB + (tc + 1) * 512],
                                start=(dc == 0), stop=(dc == 7))
                        return inst
                    th1 = T.op(h1mm, sig=True)
                    T.wait(pmask.get(("sh", b2k)))
                    def h2mm(e, i=i, isc=isc, tc=tc, b2k=b2k):
                        for dc in range(8):
                            inst = e.matmul(
                                pb[b2k][:, :],
                                swc[i][2][:, dc * 512 + isc * 128:dc * 512 + (isc + 1) * 128],
                                xbf[:, dc * TB + tc * 512:dc * TB + (tc + 1) * 512],
                                start=(dc == 0), stop=(dc == 7))
                        return inst
                    th2 = T.op(h2mm, sig=True)
                    A.wait(th1)
                    A.wait(swlast.get(("sl", isc % 2)))
                    tsl = A.op(lambda e, b1k=b1k, isc=isc: e.activation(
                        sl_sb[isc % 2][:, :], pb[b1k][:, :], AF.Silu), sig=True)
                    pmask[("sh", b1k)] = tsl
                    V.wait(tsl)
                    V.wait(th2)
                    V.wait(swlast.get("ztok"))
                    V.wait(wlast.get("w3"))
                    thh = V.op(lambda e, b2k=b2k, isc=isc, tc=tc: e.tensor_mul(
                        hh_sb[:, isc * TB + tc * 512:isc * TB + (tc + 1) * 512],
                        pb[b2k][:, :], sl_sb[isc % 2][:, :]), sig=True)
                    pmask[("sh", b2k)] = thh
                    swlast[("sl", isc % 2)] = thh
                    swlast[("hh", isc, tc)] = thh
            for tc in range(4):
                for dc in range(8):
                    bz = 4 + dc % 2
                    T.wait(pmask.get(("sh", bz)))
                    for isc in range(4):
                        T.wait(swlast.get(("hh", isc, tc)))
                    def zmm(e, i=i, tc=tc, dc=dc, bz=bz):
                        for isc in range(4):
                            inst = e.matmul(
                                pb[bz][:, :],
                                swc[i][3][:, isc * 1024 + dc * 128:isc * 1024 + (dc + 1) * 128],
                                hh_sb[:, isc * TB + tc * 512:isc * TB + (tc + 1) * 512],
                                start=(isc == 0), stop=(isc == 3))
                        return inst
                    tz = T.op(zmm, sig=True)
                    V.wait(tz)
                    V.wait(tln)
                    tza = V.op(lambda e, dc=dc, tc=tc, bz=bz: e.tensor_add(
                        res_sb[:, dc * TB + tc * 512:dc * TB + (tc + 1) * 512],
                        res_sb[:, dc * TB + tc * 512:dc * TB + (tc + 1) * 512],
                        pb[bz][:, :]), sig=True)
                    pmask[("sh", bz)] = tza
            swlast[i] = tz
            swlast["ztok"] = tz
        tok_z = tza

        SY.wait(tok_z)
        tok_dbg2 = SY.dma(dbg_r2, res_sb[:])
        V.wait(tok_dbg2)
        T.wait(tok_z)
        V.wait(wlast.get("h_read"))
        tok_ln2 = _emit_ln_stats_norm(p, nc, res_sb, xbf, sc,
                                      [pb[0], pb[1], pb[2], pb[3]], TB)
        SY.wait(tok_ln2)
        tout = SY.dma(x2out.rearrange("(dc p) t -> p dc t", p=128), resr)
        SY.wait(tout)
        p.flush()
    return nc






def build_launch_C():
    from contextlib import ExitStack
    nc = bass.Bass("TRN2", target_bir_lowering=False, debug=False,
                   detect_race_conditions=False)
    x2T = nc.dram_tensor("x2T", [D, S], F32, kind="ExternalInput").ap()
    wqTh = nc.dram_tensor("wqTh", [D, 256], F32, kind="ExternalInput").ap()
    wkTh = nc.dram_tensor("wkTh", [D, 256], F32, kind="ExternalInput").ap()
    wvTh = nc.dram_tensor("wvTh", [D, 256], F32, kind="ExternalInput").ap()
    cos2 = nc.dram_tensor("cos2", [128, S], F32, kind="ExternalInput").ap()
    sin2 = nc.dram_tensor("sin2", [128, S], F32, kind="ExternalInput").ap()
    r128T = nc.dram_tensor("r128T", [128, 128], F32, kind="ExternalInput").ap()
    maskc = nc.dram_tensor("maskc", [128, 4 * 512], F32, kind="ExternalInput").ap()
    attn1 = nc.dram_tensor("attn1", [256, S], F32, kind="ExternalOutput").ap()

    NT = S // 512
    NKC = S // 128

    with ExitStack() as ctx:
        en = ctx.enter_context
        wq_sb = en(nc.sbuf_tensor("wq_sb", [128, 8 * 256], BF16))
        wk_sb = en(nc.sbuf_tensor("wk_sb", [128, 8 * 256], BF16))
        wv_sb = en(nc.sbuf_tensor("wv_sb", [128, 8 * 256], BF16))
        x_sb = [en(nc.sbuf_tensor(f"x_sb{i}", [128, 8 * 512], BF16)) for i in range(2)]
        q2_sb = [en(nc.sbuf_tensor(f"q2_sb{i}", [128, S], BF16)) for i in range(2)]
        k2_sb = [en(nc.sbuf_tensor(f"k2_sb{i}", [128, S], BF16)) for i in range(2)]
        v_sb = en(nc.sbuf_tensor("v_sb", [128, NKC * 4 * 65], BF16))
        cos_sb = en(nc.sbuf_tensor("cos_sb", [128, S], BF16))
        sin_sb = en(nc.sbuf_tensor("sin_sb", [128, S], BF16))
        r_sb = en(nc.sbuf_tensor("r_sb", [128, 128], BF16))
        mask_sb = en(nc.sbuf_tensor("mask_sb", [128, 4 * 512], F32))
        qraw = [en(nc.sbuf_tensor(f"qraw{i}", [128, 512], BF16)) for i in range(2)]
        t1_sb = [en(nc.sbuf_tensor(f"t1_sb{i}", [128, 512], F32)) for i in range(2)]
        t2_sb = [en(nc.sbuf_tensor(f"t2_sb{i}", [128, 512], F32)) for i in range(2)]
        expst8 = en(nc.sbuf_tensor("expst8", [128, 8 * 512], BF16))
        expst = [expst8[:, i * 512:(i + 1) * 512] for i in range(8)]
        bc_sb = [en(nc.sbuf_tensor(f"bc_sb{i}", [64, 512], F32)) for i in range(2)]
        rd_sb = [en(nc.sbuf_tensor(f"rd_sb{i}", [1, 512], F32)) for i in range(2)]
        out_sb = [en(nc.sbuf_tensor(f"out_sb{i}", [64, 512], F32)) for i in range(2)]
        ones_sb = en(nc.sbuf_tensor("ones_sb", [1, 64], F32))
        ebias_sb = en(nc.sbuf_tensor("ebias_sb", [128, 1], F32))
        zpad_sb = en(nc.sbuf_tensor("zpad_sbC", [128, 512], BF16))

        stbigC = en(nc.psum_tensor("stbigC", [128, 1024], F32))
        pb = [stbigC[:, 0:512], stbigC[:, 512:1024]] + \
            [en(nc.psum_tensor(f"pb{i}", [128, 512], F32)) for i in range(2, 8)]

        blk = en(nc.Block())
        p = Prog(nc, ctx, blk)
        T, V, A, G, SY = p.tensor, p.vector, p.scalar, p.gpsimd, p.sync

        G.dma(wq_sb[:].rearrange("p (dc o) -> p dc o", o=256),
              wqTh.rearrange("(dc p) o -> p dc o", p=128))
        G.dma(wk_sb[:].rearrange("p (dc o) -> p dc o", o=256),
              wkTh.rearrange("(dc p) o -> p dc o", p=128))
        G.dma(wv_sb[:].rearrange("p (dc o) -> p dc o", o=256),
              wvTh.rearrange("(dc p) o -> p dc o", p=128))
        tok_w = G.dma(r_sb[:], r128T)
        G.dma(cos_sb[:], cos2)
        G.dma(sin_sb[:], sin2)
        tok_c = SY.dma(mask_sb[:], maskc)
        tok_cs = G.tok()
        V.wait(tok_cs)
        V.op(lambda e: e.memset(v_sb[:], 0.0))
        V.op(lambda e: e.memset(
            v_sb[:].rearrange("p (kc w) -> p kc w", w=65)[:, :, 64:65], 1.0))
        V.op(lambda e: e.memset(ebias_sb[:], EXP_BIAS))
        V.op(lambda e: e.memset(zpad_sb[:], 0.0))
        tok_ones = V.op(lambda e: e.memset(ones_sb[:], 1.0), sig=True)
        V.wait(tok_c)

        evac = {}
        for tc in range(NT):
            xb = x_sb[tc % 2]
            if tc >= 2:
                G.wait(evac.get(("x", tc % 2)))
            tok_x = G.dma(
                xb[:].rearrange("p (dc t) -> p dc t", t=512),
                x2T.rearrange("(dc p) t -> p dc t", p=128)[:, :, tc * 512:(tc + 1) * 512])
            T.wait(tok_x)
            T.wait(tok_w)
            T.wait(tok_ones)
            for which, wsb, dst in (("q", wq_sb, q2_sb), ("k", wk_sb, k2_sb)):
                for oc in range(2):
                    bank = 0 if which == "q" else 1
                    T.wait(evac.get((which, oc)))
                    def prj(e, wsb=wsb, oc=oc, xb=xb, bank=bank):
                        for dc in range(8):
                            inst = e.matmul(
                                pb[bank][:, :],
                                wsb[:, dc * 256 + oc * 128:dc * 256 + (oc + 1) * 128],
                                xb[:, dc * 512:(dc + 1) * 512],
                                start=(dc == 0), stop=(dc == 7))
                        return inst
                    tp = T.op(prj, sig=True)
                    V.wait(tp)
                    V.wait(evac.get((which, oc, "qraw")))
                    tqr = V.op(lambda e, oc=oc, bank=bank: e.tensor_copy(
                        qraw[oc][:, :], pb[bank][:, :]), sig=True)
                    T.wait(tqr)
                    T.wait(evac.get((which, oc, "rot")))
                    trm = T.op(lambda e, oc=oc: e.matmul(
                        pb[2 + oc][:, :], r_sb[:, :], qraw[oc][:, :],
                        start=True, stop=True), sig=True)
                    V.op(lambda e, oc=oc, tc=tc: e.tensor_mul(
                        t1_sb[oc][:, :], qraw[oc][:, :],
                        cos_sb[:, tc * 512:(tc + 1) * 512]))
                    V.wait(trm)
                    V.op(lambda e, oc=oc, tc=tc: e.tensor_mul(
                        t2_sb[oc][:, :], pb[2 + oc][:, :],
                        sin_sb[:, tc * 512:(tc + 1) * 512]))
                    tev = V.op(lambda e, oc=oc, tc=tc, dst=dst: e.tensor_add(
                        dst[oc][:, tc * 512:(tc + 1) * 512],
                        t1_sb[oc][:, :], t2_sb[oc][:, :]), sig=True)
                    evac[(which, oc)] = tev
                    evac[(which, oc, "qraw")] = trm
                    evac[(which, oc, "rot")] = tev
            for ts in range(4):
                kt = tc * 4 + ts
                T.wait(evac.get("v"))
                def vmm(e, xb=xb, ts=ts):
                    for dc in range(8):
                        e.matmul(pb[4][:, 0:256],
                                 xb[:, dc * 512 + ts * 128:dc * 512 + (ts + 1) * 128],
                                 wv_sb[:, dc * 256:(dc + 1) * 256],
                                 start=(dc == 0), stop=False)
                    return e.matmul(pb[4][:, 0:512],
                                    xb[:, ts * 128:(ts + 1) * 128],
                                    zpad_sb[:, 0:512], start=False, stop=True)
                tv = T.op(vmm, sig=True)
                V.wait(tv)
                tvv = None
                for h in range(4):
                    tvv = V.op(lambda e, kt=kt, h=h: e.tensor_copy(
                        v_sb[:, (kt * 4 + h) * 65:(kt * 4 + h) * 65 + 64],
                        pb[4][:, h * 64:(h + 1) * 64]), sig=(h == 3))
                evac["v"] = tvv
            evac[("x", tc % 2)] = tv

        exp_tok = {}
        pv_tok = {}
        st_tok = {}
        last_pv_use = {}
        last_bc_use = {}
        last_rd_use = {}
        store_tok = {}
        jobs = []
        for h in range(4):
            for qc in range(NT):
                jobs.append((h, qc))
        steps = []
        for j, (h, qc) in enumerate(jobs):
            for kc in range(4 * qc + 4):
                steps.append((j, h, qc, kc))

        def emit_st(gidx, h, qc, kc):
            stb = pb[gidx % 2]
            T.wait(exp_tok.get(gidx - 2))
            hp = h % 2
            ksb = k2_sb[h // 2]
            qsb = q2_sb[h // 2]
            return T.op(lambda e: e.matmul(
                stb[:, :],
                ksb[64 * hp:64 * hp + 64, kc * 128:(kc + 1) * 128],
                qsb[64 * hp:64 * hp + 64, qc * 512:(qc + 1) * 512],
                start=True, stop=True), sig=True)

        tok_p1q = evac[("q", 1)]
        tok_p1k = evac[("k", 1)]
        tok_p1v = evac["v"]
        GRP = 8
        assert len(steps) % GRP == 0
        ngrp = len(steps) // GRP
        for b in range(ngrp + 1):
            if b >= 1:
              for gg in range((b - 1) * GRP, b * GRP):
                j, h, qc, kc = steps[gg]
                pvb = pb[2 + (j % 2)]
                T.wait(exp_tok[gg])
                if kc == 0:
                    T.wait(last_pv_use.get(j % 2))
                pv_tok[gg] = T.op(lambda e, pvb=pvb, h=h, kc=kc, gg=gg: e.matmul(
                    pvb[0:65, :],
                    v_sb[:, (kc * 4 + h) * 65:(kc * 4 + h) * 65 + 65],
                    expst[gg % GRP][:, :],
                    start=(kc == 0), stop=False), sig=True)
                if kc == 4 * qc + 3:
                    tz = T.op(lambda e, pvb=pvb: e.matmul(
                        pvb[0:65, 0:512], v_sb[:, 0:65], zpad_sb[:, 0:512],
                        start=False, stop=True), sig=True)
                    pv_tok[gg] = tz
                    V.wait(tz)
                    V.wait(last_rd_use.get(j % 2))
                    trd = V.op(lambda e, j=j, pvb=pvb: e.reciprocal(
                        rd_sb[j % 2][:, :], pvb[64:65, :]), sig=True)
                    T.wait(trd)
                    T.wait(last_bc_use.get(j % 2))
                    bcb = pb[4 + (j % 2)]
                    tbc = T.op(lambda e, bcb=bcb, j=j: e.matmul(
                        bcb[0:64, :], ones_sb[0:1, 0:64], rd_sb[j % 2][:, :],
                        start=True, stop=True), sig=True)
                    last_rd_use[j % 2] = tbc
                    V.wait(tbc)
                    tcp = V.op(lambda e, bcb=bcb, j=j: e.tensor_copy(
                        bc_sb[j % 2][:, :], bcb[0:64, :]), sig=True)
                    last_bc_use[j % 2] = tcp
                    V.wait(store_tok.get(j % 2))
                    tnorm = V.op(lambda e, j=j, pvb=pvb: e.tensor_mul(
                        out_sb[j % 2][:, :], pvb[0:64, :], bc_sb[j % 2][:, :]),
                        sig=True)
                    last_pv_use[j % 2] = tnorm
                    SY.wait(tnorm)
                    store_tok[j % 2] = SY.dma(
                        attn1[64 * h:64 * (h + 1), qc * 512:(qc + 1) * 512],
                        out_sb[j % 2][:, :])
            if b < ngrp:
                for g in range(b * GRP, (b + 1) * GRP):
                    j, h, qc, kc = steps[g]
                    if g == 0:
                        T.wait(tok_p1q)
                        T.wait(tok_p1k)
                        T.wait(tok_p1v)
                        st_tok[0] = emit_st(0, h, qc, kc)
                    if g % 2 == 1:
                        A.wait(st_tok[g])
                        A.wait(pv_tok.get(g - GRP))
                        tpair = A.op(lambda e, g=g: e.activation(
                            expst8[:, ((g - 1) % GRP) * 512:((g % GRP) + 1) * 512],
                            stbigC[:, :], AF.Exp,
                            bias=ebias_sb[:, 0:1], scale=0.125), sig=True)
                        exp_tok[g] = tpair
                        exp_tok[g - 1] = tpair
                        for gm in (g - 1, g):
                            jm, hm, qcm, kcm = steps[gm]
                            m = kcm - 4 * qcm
                            if m >= 0:
                                V.wait(exp_tok[gm])
                                exp_tok[gm] = V.op(lambda e, gm=gm, m=m: e.tensor_mul(
                                    expst[gm % GRP][:, :], expst[gm % GRP][:, :],
                                    mask_sb[:, m * 512:(m + 1) * 512]), sig=True)
                    if g + 1 < len(steps):
                        jn, hn, qcn, kcn = steps[g + 1]
                        st_tok[g + 1] = emit_st(g + 1, hn, qcn, kcn)
        SY.wait(store_tok.get(0))
        SY.wait(store_tok.get(1))
        p.flush()
    return nc







def build_launch_D():
    from contextlib import ExitStack
    nc = bass.Bass("TRN2", target_bir_lowering=False, debug=False,
                   detect_race_conditions=False)
    x2Ts = nc.dram_tensor("x2Ts", [D, TB], F32, kind="ExternalInput").ap()
    attn1Ts = nc.dram_tensor("attn1Ts", [D, TB], F32, kind="ExternalInput").ap()
    woT = nc.dram_tensor("woT", [D, D], F32, kind="ExternalInput").ap()
    sw1 = nc.dram_tensor("sw1", [D, 2048], F32, kind="ExternalInput").ap()
    sw2 = nc.dram_tensor("sw2", [D, 2048], F32, kind="ExternalInput").ap()
    sw3 = nc.dram_tensor("sw3", [2048, D], F32, kind="ExternalInput").ap()
    ew1 = nc.dram_tensor("ew1", [8, D, 2048], F32, kind="ExternalInput").ap()
    ew2 = nc.dram_tensor("ew2", [8, D, 2048], F32, kind="ExternalInput").ap()
    ew3 = nc.dram_tensor("ew3", [8, 2048, D], F32, kind="ExternalInput").ap()
    rw_pk = nc.dram_tensor("rw_pk", [128, 64], F32, kind="ExternalInput").ap()
    rb_in = nc.dram_tensor("rb_in", [1, 8], F32, kind="ExternalInput").ap()
    ident_in = nc.dram_tensor("ident_in", [128, 512], F32, kind="ExternalInput").ap()
    oneh_in = nc.dram_tensor("oneh_in", [40, 8 * 128], F32, kind="ExternalInput").ap()
    outfm = nc.dram_tensor("outfm", [D, TB], F32, kind="ExternalOutput").ap()

    dram_x3 = nc.dram_tensor("dram_x3", [128, 8 * TB], F32, kind="Internal").ap()
    dram_r = nc.dram_tensor("dram_r", [128, 8 * TB], F32, kind="Internal").ap()

    TH = 512

    with ExitStack() as ctx:
        en = ctx.enter_context
        res_sb = en(nc.sbuf_tensor("res_sb", [128, 8 * TB], F32))
        stage_sb = en(nc.sbuf_tensor("stage_sb", [128, 8 * 512], F32))
        xbf = en(nc.sbuf_tensor("xbf", [128, 8 * TB], BF16))
        wa = [en(nc.sbuf_tensor(f"wa{i}", [128, 8 * 1024], BF16)) for i in range(2)]
        wb = [en(nc.sbuf_tensor(f"wb{i}", [128, 8 * 1024], BF16)) for i in range(2)]
        wc = en(nc.sbuf_tensor("wc0", [128, 8 * 1024], BF16))
        h_half = res_sb[:, 0:2048].bitcast(BF16)
        gbc_sb = res_sb[:, 2048:2560]
        u_sb = [stage_sb[:, 0:512], stage_sb[:, 512:1024]]
        t_sb = [stage_sb[:, 1024:1536], stage_sb[:, 1536:2048]]
        sl_sb = [stage_sb[:, 2048:2560], stage_sb[:, 2560:3072]]
        rww_sb = en(nc.sbuf_tensor("rww_sb", [128, 64], BF16))
        rb_sb = en(nc.sbuf_tensor("rb_sb", [1, 8], BF16))
        oneh_sb = en(nc.sbuf_tensor("oneh_sb", [40, 8 * 128], BF16))
        gs_pack = en(nc.sbuf_tensor("gs_pack", [40, TB], BF16))
        g_fm = gs_pack[0:8, :]
        sel_fm = gs_pack[32:40, :]
        sc = {}
        sc["xb1"] = [en(nc.sbuf_tensor(f"lnxb1_{i}", [128, 512], BF16)) for i in range(2)]
        sc["xb2"] = [en(nc.sbuf_tensor(f"lnxb2_{i}", [128, 512], BF16)) for i in range(2)]
        sc["mean"] = stage_sb[0:1, 3072:3584]
        sc["msq"] = stage_sb[0:1, 3584:4096]
        sc["m2"] = stage_sb[0:1, 2560:3072]
        for nm in ("nmax", "rs", "rr", "den", "rden"):
            sc[nm] = en(nc.sbuf_tensor(f"rt_{nm}", [128, 1], F32))
        for nm in ("el", "scores", "t8", "sel", "g"):
            sc[nm] = en(nc.sbuf_tensor(f"rt_{nm}", [128, 8], F32))
        sc["eps"] = en(nc.sbuf_tensor("eps_sb", [1, 1], F32))
        sc["zpad"] = en(nc.sbuf_tensor("zpad_sb", [128, 512], BF16))
        sc["g_bf"] = en(nc.sbuf_tensor("g_bf", [128, 8], BF16))
        sc["sel_bf"] = en(nc.sbuf_tensor("sel_bf", [128, 8], BF16))
        sc["identb"] = en(nc.sbuf_tensor("identb_sb", [128, 512], BF16))
        sc["onescol_bf"] = en(nc.sbuf_tensor("onescol_bf", [128, 1], BF16))
        sc["onesrow"] = en(nc.sbuf_tensor("onesrow", [1, 128], F32))
        sc["onesrow_bf"] = en(nc.sbuf_tensor("onesrow_bf", [1, 128], BF16))
        sc["ident"] = None
        sc["g_fm"] = g_fm
        sc["sel_fm"] = sel_fm

        pb = [en(nc.psum_tensor(f"pb{i}", [128, 512], F32)) for i in range(8)]
        sc["gt_ps"] = pb[6]
        sc["st_ps"] = pb[7]

        blk = en(nc.Block())
        p = Prog(nc, ctx, blk)
        T, V, A, G, SY = p.tensor, p.vector, p.scalar, p.gpsimd, p.sync

        G.dma(rww_sb[:], rw_pk)
        G.dma(rb_sb[:], rb_in)
        G.dma(sc["identb"][:], ident_in)
        tok_g0 = G.dma(oneh_sb[:], oneh_in)
        V.op(lambda e: e.memset(sc["eps"][:], 1e-5))
        V.op(lambda e: e.memset(sc["onescol_bf"][:], 1.0))
        V.op(lambda e: e.memset(sc["zpad"][:], 0.0))
        V.op(lambda e: e.memset(sc["onesrow_bf"][:], 1.0))
        tok_k = V.op(lambda e: e.memset(sc["onesrow"][:], 1.0), sig=True)
        T.wait(tok_g0)
        A.wait(tok_g0)

        resr = res_sb[:].rearrange("p (dc t) -> p dc t", t=TB)
        tok_wo = G.dma(wa[0][:].rearrange("p (dc o) -> p dc o", o=1024),
                       woT.rearrange("(dc p) o -> p dc o", p=128))
        tok_ab = G.dma(xbf[:].rearrange("p (dc t) -> p dc t", t=TB),
                       attn1Ts.rearrange("(dc p) t -> p dc t", p=128))
        tok_x2 = SY.dma(resr, x2Ts.rearrange("(dc p) t -> p dc t", p=128))
        pmask = {}
        tadd = None
        for tc in range(4):
            for dc in range(8):
                bank = dc % 2
                T.wait(tok_wo)
                T.wait(tok_ab)
                T.wait(pmask.get(("wo", bank)))
                def womm(e, dc=dc, tc=tc, bank=bank):
                    for ac in range(8):
                        inst = e.matmul(
                            pb[bank][:, :],
                            wa[0][:, ac * 1024 + dc * 128:ac * 1024 + (dc + 1) * 128],
                            xbf[:, ac * TB + tc * 512:ac * TB + (tc + 1) * 512],
                            start=(ac == 0), stop=(ac == 7))
                    return inst
                tw = T.op(womm, sig=True)
                V.wait(tw)
                V.wait(tok_x2)
                tadd = V.op(lambda e, dc=dc, tc=tc, bank=bank: e.tensor_add(
                    res_sb[:, dc * TB + tc * 512:dc * TB + (tc + 1) * 512],
                    res_sb[:, dc * TB + tc * 512:dc * TB + (tc + 1) * 512],
                    pb[bank][:, :]), sig=True)
                pmask[("wo", bank)] = tadd
        T.wait(tok_k)
        T.wait(tadd)
        tok_ln3 = _emit_ln_stats_norm(p, nc, res_sb, xbf, sc,
                                      [pb[0], pb[1], pb[2], pb[3]], TB)
        SY.wait(tok_ln3)
        tok_spill = SY.dma(dram_x3, res_sb[:])

        def rb_mm(e, logit_ps):
            return e.matmul(logit_ps[:, 0:8], sc["onesrow_bf"][0:1, 0:128],
                            rb_sb[0:1, 0:8], start=False, stop=False)
        T.wait(tok_ln3)
        tok_rt = _emit_router(p, nc, xbf, sc, pb[5], rww_sb, TB, renorm=True,
                              extra_bias_mm=rb_mm)

        def load_ab(buf, e_i, half):
            G.dma(wa[buf][:].rearrange("p (dc o) -> p dc o", o=1024),
                  ew1[e_i].rearrange("(dc p) o -> p dc o", p=128)[:, :, half * 1024:(half + 1) * 1024])
            return G.dma(wb[buf][:].rearrange("p (dc o) -> p dc o", o=1024),
                         ew2[e_i].rearrange("(dc p) o -> p dc o", p=128)[:, :, half * 1024:(half + 1) * 1024])

        def load_c(e_i, half):
            return G.dma(wc[:].rearrange("p (hc o) -> p hc o", o=1024),
                         ew3[e_i].rearrange("(hcc p) o -> p hcc o", p=128)[:, half * 8:(half + 1) * 8, :])
        G.wait(tw)
        wlast = {}
        tok_we = {}
        tok_we[0] = load_ab(0, 0, 0)
        tok_wc = load_c(0, 0)
        ydma = []
        hseq = [(e_i, half) for e_i in range(8) for half in range(2)]
        for si, (e_i, half) in enumerate(hseq):
            buf = si % 2
            if si > 0:
                G.wait(wlast.get("wc"))
                tok_wc = load_c(e_i, half)
            if si + 1 < len(hseq):
                G.wait(wlast.get(("w", (si + 1) % 2)))
                tok_we[(si + 1) % 2] = load_ab((si + 1) % 2,
                                               hseq[si + 1][0], hseq[si + 1][1])
            for th in range(4):
                T.wait(tok_rt)
                T.wait(pmask.get(("b", 6)))
                tg1 = T.op(lambda e, e_i=e_i, th=th: e.matmul(
                    pb[6][:, :], oneh_sb[0:8, e_i * 128:(e_i + 1) * 128],
                    g_fm[:, th * TH:(th + 1) * TH], start=True, stop=True), sig=True)
                V.wait(tg1)
                V.wait(wlast.get("gbc_r"))
                tgb = V.op(lambda e: e.tensor_copy(gbc_sb[:, 0:TH], pb[6][:, 0:TH]),
                           sig=True)
                pmask[("b", 6)] = tgb
                for hc in range(8):
                    b1k, b2k = hc % 2, 2 + hc % 2
                    T.wait(tok_we[buf])
                    T.wait(pmask.get(("p1", b1k)))
                    def p1mm(e, buf=buf, hc=hc, th=th, b1k=b1k):
                        for dc in range(8):
                            inst = e.matmul(
                                pb[b1k][:, :],
                                wa[buf][:, dc * 1024 + hc * 128:dc * 1024 + (hc + 1) * 128],
                                xbf[:, dc * TB + th * TH:dc * TB + (th + 1) * TH],
                                start=(dc == 0), stop=(dc == 7))
                        return inst
                    tp1 = T.op(p1mm, sig=True)
                    T.wait(pmask.get(("p2", b2k)))
                    def p2mm(e, buf=buf, hc=hc, th=th, b2k=b2k):
                        for dc in range(8):
                            inst = e.matmul(
                                pb[b2k][:, :],
                                wb[buf][:, dc * 1024 + hc * 128:dc * 1024 + (hc + 1) * 128],
                                xbf[:, dc * TB + th * TH:dc * TB + (th + 1) * TH],
                                start=(dc == 0), stop=(dc == 7))
                        return inst
                    tp2 = T.op(p2mm, sig=True)
                    A.wait(tp1)
                    A.wait(wlast.get(("sl", hc % 2)))
                    tsl = A.op(lambda e, b1k=b1k, hc=hc: e.activation(
                        sl_sb[hc % 2][:, :], pb[b1k][:, :], AF.Silu), sig=True)
                    pmask[("p1", b1k)] = tsl
                    V.wait(tsl)
                    V.wait(tp2)
                    V.wait(wlast.get("h_read"))
                    thh = V.op(lambda e, b2k=b2k, hc=hc: e.tensor_mul(
                        h_half[:, hc * TH:(hc + 1) * TH], pb[b2k][:, :],
                        sl_sb[hc % 2][:, :]), sig=True)
                    pmask[("p2", b2k)] = thh
                    wlast[("sl", hc % 2)] = thh
                    wlast[("hhw", hc)] = thh
                for dcc in range(8):
                    bank = 4 + dcc % 2
                    T.wait(pmask.get(("b", bank)))
                    for hc in range(8):
                        T.wait(wlast.get(("hhw", hc)))
                    T.wait(tok_wc)
                    def eomm(e, dcc=dcc):
                        for hc in range(8):
                            inst = e.matmul(
                                pb[4 + dcc % 2][:, :],
                                wc[:, hc * 1024 + dcc * 128:hc * 1024 + (dcc + 1) * 128],
                                h_half[:, hc * TH:(hc + 1) * TH],
                                start=(hc == 0), stop=(hc == 7))
                        return inst
                    teo = T.op(eomm, sig=True)
                    V.wait(teo)
                    V.wait(tgb)
                    V.wait(wlast.get(("tdma", dcc % 2)))
                    tpr = V.op(lambda e, bank=bank, dcc=dcc: e.tensor_mul(
                        t_sb[dcc % 2][:, :], pb[bank][:, :], gbc_sb[:, 0:TH]),
                        sig=True)
                    pmask[("b", bank)] = tpr
                    G.wait(tpr)
                    G.wait(wlast.get(("yreg", th, dcc)))
                    tok_y = G.dma_accum(
                        dram_r[:, dcc * TB + th * TH:dcc * TB + (th + 1) * TH],
                        t_sb[dcc % 2][:, :], first=(si == 0))
                    ydma.append(tok_y)
                    wlast[("tdma", dcc % 2)] = tok_y
                    wlast[("yreg", th, dcc)] = tok_y
                    wlast["h_read"] = teo
                wlast["gbc_r"] = tpr
            wlast[("w", buf)] = teo
            wlast["wc"] = teo
        tok_ylast = ydma[-1]

        SY.wait(tok_spill)
        tln = None
        for tc in range(4):
            if tln is not None:
                SY.wait(tln)
            SY.wait(tok_ylast)
            tx = SY.dma(resr[:, :, tc * 512:(tc + 1) * 512],
                        dram_x3.rearrange("p (dc t) -> p dc t", t=TB)[:, :, tc * 512:(tc + 1) * 512])
            if tln is not None:
                G.wait(tln)
            G.wait(tok_ylast)
            ty = G.dma(stage_sb[:].rearrange("p (dc t) -> p dc t", t=512),
                       dram_r.rearrange("p (dc t) -> p dc t", t=TB)[:, :, tc * 512:(tc + 1) * 512])
            V.wait(tx)
            V.wait(ty)
            for dc in range(8):
                tln = V.op(lambda e, dc=dc, tc=tc: e.tensor_add(
                    res_sb[:, dc * TB + tc * 512:dc * TB + (tc + 1) * 512],
                    res_sb[:, dc * TB + tc * 512:dc * TB + (tc + 1) * 512],
                    stage_sb[:, dc * 512:(dc + 1) * 512]), sig=(dc == 7))
        swc = [{1: wa[0][:, 0:4096], 2: wa[0][:, 4096:8192], 3: wa[1][:, 0:4096]},
               {1: wa[1][:, 4096:8192], 2: wb[0][:, 0:4096], 3: wb[0][:, 4096:8192]}]
        hh_sb = wb[1][:, 0:8192]
        G.wait(wlast[("w", 0)])
        G.wait(wlast[("w", 1)])
        tok_sw = [None, None]
        def load_sw(i, ic):
            G.dma(swc[i][1][:].rearrange("p (dc t) -> p dc t", t=512),
                  sw1.rearrange("(dc p) o -> p dc o", p=128)[:, :, ic * 512:(ic + 1) * 512])
            G.dma(swc[i][2][:].rearrange("p (dc t) -> p dc t", t=512),
                  sw2.rearrange("(dc p) o -> p dc o", p=128)[:, :, ic * 512:(ic + 1) * 512])
            return G.dma(swc[i][3][:].rearrange("p (ic o) -> p ic o", o=1024),
                         sw3.rearrange("(icc p) o -> p icc o", p=128)[:, ic * 4:(ic + 1) * 4, :])
        tok_sw[0] = load_sw(0, 0)
        swlast = {}
        tza = None
        for ic in range(4):
            i = ic % 2
            if ic + 1 < 4:
                G.wait(swlast.get((ic + 1) % 2))
                tok_sw[(ic + 1) % 2] = load_sw((ic + 1) % 2, ic + 1)
            for isc in range(4):
                for tc in range(4):
                    b1k, b2k = isc % 2, 2 + isc % 2
                    T.wait(tok_sw[i])
                    T.wait(pmask.get(("sh", b1k)))
                    def h1mm(e, i=i, isc=isc, tc=tc, b1k=b1k):
                        for dc in range(8):
                            inst = e.matmul(
                                pb[b1k][:, :],
                                swc[i][1][:, dc * 512 + isc * 128:dc * 512 + (isc + 1) * 128],
                                xbf[:, dc * TB + tc * 512:dc * TB + (tc + 1) * 512],
                                start=(dc == 0), stop=(dc == 7))
                        return inst
                    th1 = T.op(h1mm, sig=True)
                    T.wait(pmask.get(("sh", b2k)))
                    def h2mm(e, i=i, isc=isc, tc=tc, b2k=b2k):
                        for dc in range(8):
                            inst = e.matmul(
                                pb[b2k][:, :],
                                swc[i][2][:, dc * 512 + isc * 128:dc * 512 + (isc + 1) * 128],
                                xbf[:, dc * TB + tc * 512:dc * TB + (tc + 1) * 512],
                                start=(dc == 0), stop=(dc == 7))
                        return inst
                    th2 = T.op(h2mm, sig=True)
                    A.wait(th1)
                    A.wait(swlast.get(("sl", isc % 2)))
                    tsl = A.op(lambda e, b1k=b1k, isc=isc: e.activation(
                        sl_sb[isc % 2][:, :], pb[b1k][:, :], AF.Silu), sig=True)
                    pmask[("sh", b1k)] = tsl
                    V.wait(tsl)
                    V.wait(th2)
                    V.wait(swlast.get("ztok"))
                    V.wait(wlast.get(("w", 1)))
                    thh = V.op(lambda e, b2k=b2k, isc=isc, tc=tc: e.tensor_mul(
                        hh_sb[:, isc * TB + tc * 512:isc * TB + (tc + 1) * 512],
                        pb[b2k][:, :], sl_sb[isc % 2][:, :]), sig=True)
                    pmask[("sh", b2k)] = thh
                    swlast[("sl", isc % 2)] = thh
                    swlast[("hh", isc, tc)] = thh
            for tc in range(4):
                for dc in range(8):
                    bz = 4 + dc % 2
                    T.wait(pmask.get(("sh", bz)))
                    for isc in range(4):
                        T.wait(swlast.get(("hh", isc, tc)))
                    def zmm(e, i=i, tc=tc, dc=dc, bz=bz):
                        for isc in range(4):
                            inst = e.matmul(
                                pb[bz][:, :],
                                swc[i][3][:, isc * 1024 + dc * 128:isc * 1024 + (dc + 1) * 128],
                                hh_sb[:, isc * TB + tc * 512:isc * TB + (tc + 1) * 512],
                                start=(isc == 0), stop=(isc == 3))
                        return inst
                    tz = T.op(zmm, sig=True)
                    V.wait(tz)
                    V.wait(tln)
                    tza = V.op(lambda e, dc=dc, tc=tc, bz=bz: e.tensor_add(
                        res_sb[:, dc * TB + tc * 512:dc * TB + (tc + 1) * 512],
                        res_sb[:, dc * TB + tc * 512:dc * TB + (tc + 1) * 512],
                        pb[bz][:, :]), sig=True)
                    pmask[("sh", bz)] = tza
            swlast[i] = tz
            swlast["ztok"] = tz
        tok_z = tza

        T.wait(tok_z)
        tok_ln4 = _emit_ln_stats_norm(p, nc, res_sb, xbf, sc,
                                      [pb[0], pb[1], pb[2], pb[3]], TB)
        SY.wait(tok_ln4)
        tout = SY.dma(outfm.rearrange("(dc p) t -> p dc t", p=128), resr)
        SY.wait(tout)
        p.flush()
    return nc



_BF = ml_dtypes.bfloat16


def _t(a):
    return np.ascontiguousarray(np.asarray(a).T.astype(np.float32))


_cache = {}


def _launch_A(inputs):
    if "A" not in _cache:
        _cache["A"] = build_launch_A()
    nc = _cache["A"]
    x = np.asarray(inputs["L0_attn_x"], np.float32)
    wq = np.asarray(inputs["L0_attn_Wq"], np.float32)
    wk = np.asarray(inputs["L0_attn_Wk"], np.float32)
    wv = np.asarray(inputs["L0_attn_Wv"], np.float32)
    wqT = _t(wq)
    wkT = _t(wk)
    wvT = _t(wv)
    in_maps = []
    for c in range(NCORES):
        b = c // 4
        hs = 4 * (c % 4)
        in_maps.append({
            "xbT": _t(x[b]),
            "wqT": np.ascontiguousarray(wqT[:, hs * DH:(hs + 4) * DH]),
            "wkT": wkT,
            "wvT": wvT,
        })
    res = run_bass_kernel_spmd(nc, in_maps, core_ids=list(range(NCORES)))
    attn_fm = np.zeros((B, D, S), np.float32)
    for c in range(NCORES):
        b = c // 4
        hs = 4 * (c % 4)
        attn_fm[b, hs * DH:(hs + 4) * DH, :] = res.results[c]["attn"]
    return attn_fm


def _launch_B(inputs, attn_fm):
    if "B" not in _cache:
        _cache["B"] = build_launch_B()
    nc = _cache["B"]
    x = np.asarray(inputs["L0_attn_x"], np.float32)
    w1 = np.asarray(inputs["L0_ffn_w1"], np.float32)
    w2 = np.asarray(inputs["L0_ffn_w2"], np.float32)
    w3 = np.asarray(inputs["L0_ffn_w3"], np.float32)
    w1T = np.ascontiguousarray(np.transpose(w1, (0, 2, 1)))
    w2T = np.ascontiguousarray(np.transpose(w2, (0, 2, 1)))
    w3T = np.ascontiguousarray(np.transpose(w3, (0, 2, 1)))
    sw1T = _t(inputs["L0_ffn_sw1"])
    sw2T = _t(inputs["L0_ffn_sw2"])
    sw3T = _t(inputs["L0_ffn_sw3"])
    gate = np.asarray(inputs["L0_ffn_gate_w"], np.float32)
    gate_pk = np.ascontiguousarray(
        gate.reshape(8, 8, 128).transpose(2, 1, 0).reshape(128, 64))
    bias_pk = np.zeros((128, 192), np.float32)
    for j, nm in enumerate(("b1", "b2", "b3")):
        bj = np.asarray(inputs[f"L0_ffn_{nm}"], np.float32).reshape(8, 8, 128)
        bias_pk[:, :] += 0.0
        for e in range(8):
            for c in range(8):
                bias_pk[:, (e * 3 + j) * 8 + c] = bj[e, c, :]
    ident = np.zeros((128, 512), np.float32)
    ident[:, 0:128] = np.eye(128, dtype=np.float32)
    oneh = np.zeros((40, 8 * 128), np.float32)
    for e in range(8):
        oneh[e, e * 128:(e + 1) * 128] = 1.0
        oneh[32 + e, e * 128:(e + 1) * 128] = 1.0
    xT = [None] * NCORES
    in_maps = []
    for c in range(NCORES):
        b = c // 4
        t0 = (c % 4) * TB
        in_maps.append({
            "xT": np.ascontiguousarray(_t(x[b])[:, t0:t0 + TB]),
            "attnT": np.ascontiguousarray(attn_fm[b][:, t0:t0 + TB]),
            "w1T": w1T, "w2T": w2T, "w3T": w3T,
            "sw1T": sw1T, "sw2T": sw2T, "sw3T": sw3T,
            "gate_pk": gate_pk, "bias_pk": bias_pk,
            "ident_in": ident, "oneh_in": oneh,
        })
    res = run_bass_kernel_spmd(nc, in_maps, core_ids=list(range(NCORES)))
    x2_fm = np.zeros((B, D, S), np.float32)
    for c in range(NCORES):
        b = c // 4
        t0 = (c % 4) * TB
        x2_fm[b][:, t0:t0 + TB] = res.results[c]["x2out"]
    return x2_fm


def _rope_tables():
    inv = 1.0 / (10000.0 ** (np.arange(0, 64, 2, dtype=np.float32) / 64.0))
    t = np.arange(S, dtype=np.float32)
    fr = np.outer(t, inv).astype(np.float32)
    emb = np.concatenate([fr, fr], axis=1)
    cos = np.cos(emb).astype(np.float32)
    sin = np.sin(emb).astype(np.float32)
    c2 = np.zeros((128, S), np.float32)
    s2 = np.zeros((128, S), np.float32)
    for p in range(128):
        c2[p] = cos[:, p % 64]
        s2[p] = sin[:, p % 64]
    return c2, s2


def _launch_C(inputs, x2_fm):
    if "C" not in _cache:
        _cache["C"] = build_launch_C()
    nc = _cache["C"]
    wq = np.asarray(inputs["L1_attn_Wq"], np.float32)
    wk = np.asarray(inputs["L1_attn_Wk"], np.float32)
    wv = np.asarray(inputs["L1_attn_Wv"], np.float32)
    c2, s2 = _rope_tables()
    R = np.zeros((64, 64), np.float32)
    for i in range(32):
        R[i, i + 32] = -1.0
    for i in range(32, 64):
        R[i, i - 32] = 1.0
    RT = np.ascontiguousarray(R.T)
    r128 = np.zeros((128, 128), np.float32)
    r128[0:64, 0:64] = RT
    r128[64:128, 64:128] = RT
    maskc = np.zeros((128, 4 * 512), np.float32)
    for m in range(4):
        for i_ in range(128):
            jj = np.arange(512)
            maskc[i_, m * 512:(m + 1) * 512] = (i_ <= jj - 128 * m).astype(np.float32)
    in_maps = []
    for c in range(NCORES):
        b = c // 4
        hs = 4 * (c % 4)
        in_maps.append({
            "x2T": np.ascontiguousarray(x2_fm[b]),
            "wqTh": np.ascontiguousarray(_t(wq)[:, hs * 64:(hs + 4) * 64]),
            "wkTh": np.ascontiguousarray(_t(wk)[:, hs * 64:(hs + 4) * 64]),
            "wvTh": np.ascontiguousarray(_t(wv)[:, hs * 64:(hs + 4) * 64]),
            "cos2": c2, "sin2": s2, "r128T": r128, "maskc": maskc,
        })
    res = run_bass_kernel_spmd(nc, in_maps, core_ids=list(range(NCORES)))
    attn1_fm = np.zeros((B, D, S), np.float32)
    for c in range(NCORES):
        b = c // 4
        hs = 4 * (c % 4)
        attn1_fm[b, hs * 64:(hs + 4) * 64, :] = res.results[c]["attn1"]
    return attn1_fm


def _launch_D(inputs, x2_fm, attn1_fm):
    if "Dl" not in _cache:
        _cache["Dl"] = build_launch_D()
    nc = _cache["Dl"]
    wo = np.asarray(inputs["L1_attn_Wo"], np.float32)
    woT = _t(wo)
    sw1 = np.ascontiguousarray(np.asarray(inputs["L1_ffn_sw1"], np.float32))
    sw2 = np.ascontiguousarray(np.asarray(inputs["L1_ffn_sw2"], np.float32))
    sw3 = np.ascontiguousarray(np.asarray(inputs["L1_ffn_sw3"], np.float32))
    ew1 = np.ascontiguousarray(np.asarray(inputs["L1_ffn_ew1"], np.float32))
    ew2 = np.ascontiguousarray(np.asarray(inputs["L1_ffn_ew2"], np.float32))
    ew3 = np.ascontiguousarray(np.asarray(inputs["L1_ffn_ew3"], np.float32))
    rw = np.asarray(inputs["L1_ffn_rw"], np.float32)
    rb = np.asarray(inputs["L1_ffn_rb"], np.float32).reshape(1, 8)
    rw_pk = np.ascontiguousarray(
        rw.reshape(8, 128, 8).transpose(1, 0, 2).reshape(128, 64))
    ident = np.zeros((128, 512), np.float32)
    ident[:, 0:128] = np.eye(128, dtype=np.float32)
    oneh = np.zeros((40, 8 * 128), np.float32)
    for e in range(8):
        oneh[e, e * 128:(e + 1) * 128] = 1.0
        oneh[32 + e, e * 128:(e + 1) * 128] = 1.0
    in_maps = []
    for c in range(NCORES):
        b = c // 4
        t0 = (c % 4) * TB
        in_maps.append({
            "x2Ts": np.ascontiguousarray(x2_fm[b][:, t0:t0 + TB]),
            "attn1Ts": np.ascontiguousarray(attn1_fm[b][:, t0:t0 + TB]),
            "woT": woT, "sw1": sw1, "sw2": sw2, "sw3": sw3,
            "ew1": ew1, "ew2": ew2, "ew3": ew3,
            "rw_pk": rw_pk, "rb_in": rb,
            "ident_in": ident, "oneh_in": oneh,
        })
    res = run_bass_kernel_spmd(nc, in_maps, core_ids=list(range(NCORES)))
    out = np.zeros((B, S, D), np.float32)
    for c in range(NCORES):
        b = c // 4
        t0 = (c % 4) * TB
        out[b, t0:t0 + TB, :] = res.results[c]["outfm"].T
    return out


def kernel(**inputs):
    attn_fm = _launch_A(inputs)
    x2_fm = _launch_B(inputs, attn_fm)
    attn1_fm = _launch_C(inputs, x2_fm)
    out = _launch_D(inputs, x2_fm, attn1_fm)
    return out


# version 08 (final)
# speedup vs baseline: 1.2066x; 1.2066x over previous
import numpy as np
import ml_dtypes
import concourse.bass as bass
import concourse.mybir as mybir
from concourse.bass_utils import run_bass_kernel_spmd

F32 = mybir.dt.float32
BF16 = mybir.dt.bfloat16
AF = mybir.ActivationFunctionType
ALU = mybir.AluOpType
AX = mybir.AxisListType

B, S, D = 2, 8192, 1024
NH, DH = 16, 64
NCORES = 8
EXP_BIAS = -8.0

ENGINES = ("tensor", "vector", "scalar", "gpsimd", "sync")


class _EngQ:

    def __init__(self, prog, name, sem):
        self.prog = prog
        self.name = name
        self.sem = sem
        self.cnt = 0
        self.q = []
        self._waited = {}

    def wait(self, tok):
        if tok is None:
            return
        sem, val = tok
        key = id(sem)
        if self._waited.get(key, -1) >= val:
            return
        self._waited[key] = val
        self.q.append(lambda e: e.wait_ge(sem, val))

    def op(self, fn, sig=False, inc=1):
        if sig:
            self.cnt += inc
            sem = self.sem
            self.q.append(lambda e: fn(e).then_inc(sem, inc))
            return (sem, self.cnt)
        self.q.append(fn)
        return None

    def dma(self, out, in_):
        self.cnt += 16
        sem = self.sem
        self.q.append(lambda e: e.dma_start(out=out, in_=in_).then_inc(sem, 16))
        return (sem, self.cnt)

    def dma_accum(self, out, in_, first):
        self.cnt += 16
        sem = self.sem
        op = ALU.bypass if first else ALU.add
        self.q.append(lambda e: e.dma_start(out=out, in_=in_, accum_op=op)
                      .then_inc(sem, 16))
        return (sem, self.cnt)

    def tok(self):
        return (self.sem, self.cnt) if self.cnt else None


class Prog:
    def __init__(self, nc, ctx, blk):
        self.nc = nc
        self.blk = blk
        self.e = {}
        for name in ENGINES:
            sem = ctx.enter_context(nc.semaphore(f"sem_{name}"))
            self.e[name] = _EngQ(self, name, sem)
        self.tensor = self.e["tensor"]
        self.vector = self.e["vector"]
        self.scalar = self.e["scalar"]
        self.gpsimd = self.e["gpsimd"]
        self.sync = self.e["sync"]

    def flush(self):
        for name in ENGINES:
            q = self.e[name].q
            if not q:
                continue
            def runner(eng, _q=list(q)):
                for fn in _q:
                    fn(eng)
            getattr(self.blk, name)(runner)
            self.e[name].q = []



def build_launch_A():
    from contextlib import ExitStack
    nc = bass.Bass("TRN2", target_bir_lowering=False, debug=False,
                   detect_race_conditions=False)
    xbT = nc.dram_tensor("xbT", [D, S], F32, kind="ExternalInput").ap()
    wqT = nc.dram_tensor("wqT", [D, 256], F32, kind="ExternalInput").ap()
    wkT = nc.dram_tensor("wkT", [D, DH], F32, kind="ExternalInput").ap()
    wvT = nc.dram_tensor("wvT", [D, DH], F32, kind="ExternalInput").ap()
    attn = nc.dram_tensor("attn", [256, S], F32, kind="ExternalOutput").ap()

    NT = S // 512
    NKC = S // 128

    with ExitStack() as ctx:
        en = ctx.enter_context
        wq_sb = en(nc.sbuf_tensor("wq_sb", [128, 8 * 256], BF16))
        wk_sb = en(nc.sbuf_tensor("wk_sb", [128, 8 * DH], BF16))
        wv_sb = en(nc.sbuf_tensor("wv_sb", [128, 8 * DH], BF16))
        x_sb = [en(nc.sbuf_tensor(f"x_sb{i}", [128, 8 * 512], BF16)) for i in range(2)]
        qh_sb = [en(nc.sbuf_tensor(f"qh_sb{i}", [64, S], BF16)) for i in range(4)]
        k_sb = en(nc.sbuf_tensor("k_sb", [64, S], BF16))
        v_sb = en(nc.sbuf_tensor("v_sb", [128, NKC * 65], BF16))
        expb = [en(nc.sbuf_tensor(f"expb{i}", [128, 16 * 1024], BF16)) for i in range(2)]
        bc_sb = [en(nc.sbuf_tensor(f"bc_sb{i}", [64, 512], F32)) for i in range(2)]
        rd_sb = [en(nc.sbuf_tensor(f"rd_sb{i}", [1, 512], F32)) for i in range(2)]
        out_sb = [en(nc.sbuf_tensor(f"out_sb{i}", [64, 1024], F32)) for i in range(2)]
        ones_sb = en(nc.sbuf_tensor("ones_sb", [1, 64], F32))
        ebias_sb = en(nc.sbuf_tensor("ebias_sb", [128, 1], F32))
        zpad_sb = en(nc.sbuf_tensor("zpad_sbA", [128, 512], BF16))

        stp = [en(nc.psum_tensor(f"stp{i}", [128, 1024], F32)) for i in range(2)]
        pvp = [en(nc.psum_tensor(f"pvp{i}", [128, 512], F32)) for i in range(2)]
        bcp = [en(nc.psum_tensor(f"bcp{i}", [128, 512], F32)) for i in range(2)]

        blk = en(nc.Block())
        p = Prog(nc, ctx, blk)
        T, V, A, G, SY = p.tensor, p.vector, p.scalar, p.gpsimd, p.sync

        tw = []
        tw.append(G.dma(wq_sb[:].rearrange("p (dc o) -> p dc o", o=256),
                        wqT.rearrange("(dc p) o -> p dc o", p=128)))
        tw.append(G.dma(wk_sb[:].rearrange("p (dc o) -> p dc o", o=DH),
                        wkT.rearrange("(dc p) o -> p dc o", p=128)))
        tw.append(G.dma(wv_sb[:].rearrange("p (dc o) -> p dc o", o=DH),
                        wvT.rearrange("(dc p) o -> p dc o", p=128)))
        tok_w = tw[-1]
        V.op(lambda e: e.memset(v_sb[:], 0.0))
        V.op(lambda e: e.memset(
            v_sb[:].rearrange("p (kc w) -> p kc w", w=65)[:, :, 64:65], 1.0))
        V.op(lambda e: e.memset(ebias_sb[:], EXP_BIAS))
        V.op(lambda e: e.memset(zpad_sb[:], 0.0))
        tok_ones = V.op(lambda e: e.memset(ones_sb[:], 1.0), sig=True)

        evac_tok = {}
        for tc in range(NT):
            xb = x_sb[tc % 2]
            if tc >= 2:
                G.wait(evac_tok.get(("x", tc % 2)))
            tok_x = G.dma(
                xb[:].rearrange("p (dc t) -> p dc t", t=512),
                xbT.rearrange("(dc p) t -> p dc t", p=128)[:, :, tc * 512:(tc + 1) * 512])
            T.wait(tok_x)
            T.wait(tok_w)
            T.wait(tok_ones)
            for oc in range(2):
                qv = stp[0][:, oc * 512:(oc + 1) * 512]
                T.wait(evac_tok.get(oc))
                def qmm(e, oc=oc, xb=xb, qv=qv):
                    for dc in range(8):
                        inst = e.matmul(qv,
                                        wq_sb[:, dc * 256 + oc * 128:dc * 256 + (oc + 1) * 128],
                                        xb[:, dc * 512:(dc + 1) * 512],
                                        start=(dc == 0), stop=(dc == 7))
                    return inst
                tq = T.op(qmm, sig=True)
                V.wait(tq)
                V.op(lambda e, oc=oc, tc=tc, qv=qv: e.tensor_copy(
                    qh_sb[2 * oc][:, tc * 512:(tc + 1) * 512], qv[0:64, :]))
                tev = V.op(lambda e, oc=oc, tc=tc, qv=qv: e.tensor_copy(
                    qh_sb[2 * oc + 1][:, tc * 512:(tc + 1) * 512], qv[64:128, :]),
                    sig=True)
                evac_tok[oc] = tev
            kv = stp[1][:, 0:512]
            T.wait(evac_tok.get(2))
            def kmm(e, xb=xb, kv=kv):
                for dc in range(8):
                    inst = e.matmul(kv[0:64, :],
                                    wk_sb[:, dc * DH:(dc + 1) * DH],
                                    xb[:, dc * 512:(dc + 1) * 512],
                                    start=(dc == 0), stop=(dc == 7))
                return inst
            tk = T.op(kmm, sig=True)
            V.wait(tk)
            tev = V.op(lambda e, tc=tc, kv=kv: e.tensor_copy(
                k_sb[:, tc * 512:(tc + 1) * 512], kv[0:64, :]), sig=True)
            evac_tok[2] = tev
            vv_ps = stp[1][:, 512:1024]
            for ts in range(4):
                kt = tc * 4 + ts
                T.wait(evac_tok.get(3))
                def vmm(e, xb=xb, ts=ts, vv_ps=vv_ps):
                    for dc in range(8):
                        e.matmul(vv_ps[:, 0:DH],
                                 xb[:, dc * 512 + ts * 128:dc * 512 + (ts + 1) * 128],
                                 wv_sb[:, dc * DH:(dc + 1) * DH],
                                 start=(dc == 0), stop=False)
                    return e.matmul(vv_ps[:, 0:512],
                                    xb[:, ts * 128:(ts + 1) * 128],
                                    zpad_sb[:, 0:512], start=False, stop=True)
                tv = T.op(vmm, sig=True)
                V.wait(tv)
                tev = V.op(lambda e, kt=kt, vv_ps=vv_ps: e.tensor_copy(
                    v_sb[:, kt * 65:kt * 65 + 64], vv_ps[:, 0:DH]), sig=True)
                evac_tok[3] = tev
            evac_tok[("x", tc % 2)] = tv

        tok_p1 = evac_tok[3]
        T.wait(tok_p1)
        NQ2 = S // 1024
        KBLK = 16
        jobs = [(h, qc) for h in range(4) for qc in range(NQ2)]
        NB = len(jobs) * 4
        last_pv_use = {}
        last_bc_use = {}
        last_rd_use = {}
        store_tok = {}
        exp_tok = {}
        exp_blk = {}
        pv_blk = {}
        st_tok = {}

        def tail(j):
            h, qc = jobs[j]
            tpv = pv_blk[4 * j + 3]
            V.wait(tpv)
            tnorm = None
            for half, pvb, bcb, key in ((0, pvp[0], bcp[0], "a"),
                                        (1, pvp[1], bcp[1], "b")):
                V.wait(last_rd_use.get(key))
                trd = V.op(lambda e, half=half, pvb=pvb: e.reciprocal(
                    rd_sb[half][:, :], pvb[64:65, :]), sig=True)
                T.wait(trd)
                T.wait(last_bc_use.get(key))
                tbc = T.op(lambda e, bcb=bcb, half=half: e.matmul(
                    bcb[0:64, :], ones_sb[0:1, 0:64], rd_sb[half][:, :],
                    start=True, stop=True), sig=True)
                last_rd_use[key] = tbc
                V.wait(tbc)
                tcp = V.op(lambda e, bcb=bcb, half=half: e.tensor_copy(
                    bc_sb[half][:, :], bcb[0:64, :]), sig=True)
                last_bc_use[key] = tcp
                V.wait(store_tok.get(j % 2))
                tnorm = V.op(lambda e, half=half, pvb=pvb, j=j: e.tensor_mul(
                    out_sb[j % 2][:, half * 512:(half + 1) * 512],
                    pvb[0:64, :], bc_sb[half][:, :]), sig=True)
                last_pv_use[key] = tnorm
            SY.wait(tnorm)
            store_tok[j % 2] = SY.dma(
                attn[64 * h:64 * (h + 1), qc * 1024:(qc + 1) * 1024],
                out_sb[j % 2][:, :])

        for Bl in range(NB + 1):
            if Bl >= 1:
                Bp = Bl - 1
                T.wait(exp_blk[Bp])
                for kk in range(KBLK):
                    kcp = (Bp % 4) * KBLK + kk
                    if kcp == 0:
                        T.wait(last_pv_use.get("a"))
                        T.wait(last_pv_use.get("b"))
                    def pvf(e, Bp=Bp, kk=kk, kcp=kcp):
                        e.matmul(pvp[0][0:65, :],
                                 v_sb[:, kcp * 65:(kcp + 1) * 65],
                                 expb[Bp % 2][:, kk * 1024:kk * 1024 + 512],
                                 start=(kcp == 0), stop=(kcp == NKC - 1))
                        return e.matmul(pvp[1][0:65, :],
                                        v_sb[:, kcp * 65:(kcp + 1) * 65],
                                        expb[Bp % 2][:, kk * 1024 + 512:(kk + 1) * 1024],
                                        start=(kcp == 0), stop=(kcp == NKC - 1))
                    tpv = T.op(pvf, sig=(kk == KBLK - 1))
                    if kk == KBLK - 1:
                        pv_blk[Bp] = tpv
            if Bl < NB:
                j = Bl // 4
                h, qc = jobs[j]
                qsb = qh_sb[h]
                for kk in range(KBLK):
                    kc = (Bl % 4) * KBLK + kk
                    g = Bl * KBLK + kk
                    stt = stp[g % 2]
                    T.wait(exp_tok.get(g - 2))
                    def stf(e, stt=stt, kc=kc, qsb=qsb, qc=qc):
                        e.matmul(stt[:, 0:512],
                                 k_sb[:, kc * 128:(kc + 1) * 128],
                                 qsb[:, qc * 1024:qc * 1024 + 512],
                                 start=True, stop=True)
                        return e.matmul(stt[:, 512:1024],
                                        k_sb[:, kc * 128:(kc + 1) * 128],
                                        qsb[:, qc * 1024 + 512:(qc + 1) * 1024],
                                        start=True, stop=True)
                    st_tok[g] = T.op(stf, sig=True)
                    A.wait(st_tok[g])
                    A.wait(pv_blk.get(Bl - 2))
                    exp_tok[g] = A.op(lambda e, Bl=Bl, kk=kk, g=g: e.activation(
                        expb[Bl % 2][:, kk * 1024:(kk + 1) * 1024],
                        stp[g % 2][:, :], AF.Exp,
                        bias=ebias_sb[:, 0:1], scale=0.125), sig=True)
                    if kk == KBLK - 1:
                        exp_blk[Bl] = exp_tok[g]
            if Bl >= 1 and Bl % 4 == 0:
                tail(Bl // 4 - 1)
        tail(len(jobs) - 1)
        SY.wait(store_tok.get(0))
        SY.wait(store_tok.get(1))
        p.flush()
    return nc




TB = 2048


def _emit_ln_stats_norm(p, nc, src_sb, dst_bf, sc, pbank, T_tok):
    T, V, A = p.tensor, p.vector, p.scalar
    sum_ps, ssq_ps, bc1_ps, bc2_ps = pbank
    ntc = T_tok // 512
    prev_mm = None
    tlast = None
    for tc in range(ntc):
        for dc in range(8):
            col = dc * T_tok + tc * 512
            i = dc % 2
            V.wait(prev_mm if dc >= 2 or tc > 0 else None)
            V.op(lambda e, col=col, i=i: e.tensor_copy(
                sc["xb1"][i][:, 0:512], src_sb[:, col:col + 512]))
            tsq = V.op(lambda e, col=col, i=i: e.tensor_mul(
                sc["xb2"][i][:, 0:512], src_sb[:, col:col + 512],
                src_sb[:, col:col + 512]), sig=True)
            T.wait(tsq)
            T.op(lambda e, dc=dc, i=i: e.matmul(
                sum_ps[0:1, :], sc["onescol_bf"][:, 0:1], sc["xb1"][i][:, 0:512],
                start=(dc == 0), stop=(dc == 7)))
            prev_mm = T.op(lambda e, dc=dc, i=i: e.matmul(
                ssq_ps[0:1, :], sc["onescol_bf"][:, 0:1], sc["xb2"][i][:, 0:512],
                start=(dc == 0), stop=(dc == 7)), sig=True)
        ssum_tok = prev_mm
        V.wait(ssum_tok)
        V.op(lambda e: e.tensor_scalar_mul(sc["mean"][:, :], sum_ps[0:1, :], 1.0 / D))
        V.op(lambda e: e.tensor_scalar_mul(sc["msq"][:, :], ssq_ps[0:1, :], 1.0 / D))
        V.op(lambda e: e.tensor_mul(sc["m2"][:, :], sc["mean"][:, :], sc["mean"][:, :]))
        tvar = V.op(lambda e: e.tensor_sub(sc["msq"][:, :], sc["msq"][:, :],
                                           sc["m2"][:, :]), sig=True)
        A.wait(tvar)
        A.op(lambda e: e.activation(sc["m2"][:, :], sc["msq"][:, :], AF.Ln,
                                    bias=sc["eps"][0:1, 0:1]))
        trs = A.op(lambda e: e.activation(sc["msq"][:, :], sc["m2"][:, :], AF.Exp,
                                          scale=-0.5), sig=True)
        V.wait(trs)
        tmr = V.op(lambda e: e.tensor_mul(sc["m2"][:, :], sc["mean"][:, :],
                                          sc["msq"][:, :]), sig=True)
        T.wait(tmr)
        tb1 = T.op(lambda e: e.matmul(bc1_ps[:, :], sc["onesrow"][0:1, 0:128],
                                      sc["msq"][:, :], start=True, stop=True),
                   sig=True)
        tb2 = T.op(lambda e: e.matmul(bc2_ps[:, :], sc["onesrow"][0:1, 0:128],
                                      sc["m2"][:, :], start=True, stop=True),
                   sig=True)
        V.wait(tb1)
        V.wait(tb2)
        tlast = None
        for dc in range(8):
            col = dc * T_tok + tc * 512
            V.op(lambda e, col=col: e.tensor_mul(
                src_sb[:, col:col + 512], src_sb[:, col:col + 512], bc1_ps[:, :]))
            V.op(lambda e, col=col: e.tensor_sub(
                src_sb[:, col:col + 512], src_sb[:, col:col + 512], bc2_ps[:, :]))
            tlast = V.op(lambda e, col=col: e.tensor_copy(
                dst_bf[:, col:col + 512], src_sb[:, col:col + 512]),
                sig=(dc == 7))
        T.wait(tlast)
    return tlast


def _vbar(p, sc):
    V = p.vector
    t = V.op(lambda e: e.tensor_add(sc["bar"][:, :], sc["bar"][:, :],
                                    sc["bar"][:, :]), sig=True)
    return t


def _emit_router(p, nc, xbf, sc, logit_ps, gatew_sb, T_tok, renorm,
                 extra_bias_mm=None):
    T, V, A = p.tensor, p.vector, p.scalar
    ntile = T_tok // 128
    tok_g = None
    for tt in range(ntile):
        has_bias = extra_bias_mm is not None
        def lmm(e, tt=tt, has_bias=has_bias):
            for dc in range(8):
                inst = e.matmul(logit_ps[:, 0:8],
                                xbf[:, dc * T_tok + tt * 128: dc * T_tok + (tt + 1) * 128],
                                gatew_sb[:, dc * 8:(dc + 1) * 8],
                                start=(dc == 0), stop=False)
            return inst
        T.op(lmm)
        if has_bias:
            T.op(lambda e: extra_bias_mm(e, logit_ps))
        tl = T.op(lambda e, tt=tt: e.matmul(
            logit_ps[:, 0:512],
            xbf[:, tt * 128:(tt + 1) * 128],
            sc["zpad"][:, 0:512], start=False, stop=True), sig=True)
        V.wait(tl)
        tnm = V.op(lambda e: e.reduce_max(sc["nmax"][:, :], logit_ps[:, 0:8],
                                          axis=AX.X, negate=True), sig=True)
        A.wait(tnm)
        tex = A.op(lambda e: e.activation(sc["el"][:, :], logit_ps[:, 0:8], AF.Exp,
                                          bias=sc["nmax"][:, 0:1]), sig=True)
        V.wait(tex)
        trsum = V.op(lambda e: e.reduce_sum(sc["rs"][:, :], sc["el"][:, :],
                                            axis=AX.X), sig=True)
        A.wait(trsum)
        A.op(lambda e: e.activation(sc["rr"][:, :], sc["rs"][:, :], AF.Ln))
        trr = A.op(lambda e: e.activation(sc["rr"][:, :], sc["rr"][:, :], AF.Exp,
                                          scale=-1.0), sig=True)
        V.wait(trr)
        tsc = V.op(lambda e: e.tensor_scalar_mul(sc["scores"][:, :], sc["el"][:, :],
                                                 sc["rr"][:, 0:1]), sig=True)
        V.wait(tsc)
        tmx = V.op(lambda e: e.max(out=sc["t8"][:, :], in_=sc["scores"][:, :]),
                   sig=True)
        V.wait(tmx)
        tse = V.op(lambda e: e.tensor_single_scalar(sc["sel"][:, :], sc["scores"][:, :],
                                                    sc["t8"][:, 1:2], ALU.is_ge),
                   sig=True)
        V.wait(tse)
        tg = V.op(lambda e: e.tensor_mul(sc["g"][:, :], sc["scores"][:, :],
                                         sc["sel"][:, :]), sig=True)
        if renorm:
            tden = V.op(lambda e: e.tensor_add(sc["den"][:, :], sc["t8"][:, 0:1],
                                               sc["t8"][:, 1:2]), sig=True)
            A.wait(tden)
            A.op(lambda e: e.activation(sc["rden"][:, :], sc["den"][:, :], AF.Ln))
            trd = A.op(lambda e: e.activation(sc["rden"][:, :], sc["rden"][:, :],
                                              AF.Exp, scale=-1.0), sig=True)
            V.wait(trd)
            V.wait(tg)
            tg = V.op(lambda e: e.tensor_scalar_mul(sc["g"][:, :], sc["g"][:, :],
                                                    sc["rden"][:, 0:1]), sig=True)
        V.wait(tg)
        tgc = V.op(lambda e: e.tensor_copy(sc["g_bf"][:, :], sc["g"][:, :]),
                   sig=True)
        V.wait(tgc)
        tsc2 = V.op(lambda e: e.tensor_copy(sc["sel_bf"][:, :], sc["sel"][:, :]),
                    sig=True)
        T.wait(tsc2)
        ttr = T.op(lambda e: e.matmul(sc["gt_ps"][0:8, 0:512], sc["g_bf"][:, :],
                                      sc["identb"][:, 0:512], start=True, stop=True),
                   sig=True)
        ttr2 = T.op(lambda e: e.matmul(sc["st_ps"][0:8, 0:512], sc["sel_bf"][:, :],
                                       sc["identb"][:, 0:512], start=True, stop=True),
                    sig=True)
        V.wait(ttr2)
        V.op(lambda e, tt=tt: e.tensor_copy(
            sc["g_fm"][:, tt * 128:(tt + 1) * 128], sc["gt_ps"][0:8, 0:128]))
        tok_g = V.op(lambda e, tt=tt: e.tensor_copy(
            sc["sel_fm"][:, tt * 128:(tt + 1) * 128], sc["st_ps"][0:8, 0:128]),
            sig=True)
        T.wait(tok_g)
    return tok_g


def build_launch_B():
    from contextlib import ExitStack
    nc = bass.Bass("TRN2", target_bir_lowering=False, debug=False,
                   detect_race_conditions=False)
    xT = nc.dram_tensor("xT", [D, TB], F32, kind="ExternalInput").ap()
    attnT = nc.dram_tensor("attnT", [D, TB], F32, kind="ExternalInput").ap()
    w1T = nc.dram_tensor("w1T", [8, D, D], F32, kind="ExternalInput").ap()
    w2T = nc.dram_tensor("w2T", [8, D, D], F32, kind="ExternalInput").ap()
    w3T = nc.dram_tensor("w3T", [8, D, D], F32, kind="ExternalInput").ap()
    sw1T = nc.dram_tensor("sw1T", [D, 2048], F32, kind="ExternalInput").ap()
    sw2T = nc.dram_tensor("sw2T", [D, 2048], F32, kind="ExternalInput").ap()
    sw3T = nc.dram_tensor("sw3T", [2048, D], F32, kind="ExternalInput").ap()
    gate_pk = nc.dram_tensor("gate_pk", [128, 64], F32, kind="ExternalInput").ap()
    bias_pk = nc.dram_tensor("bias_pk", [128, 192], F32, kind="ExternalInput").ap()
    ident_in = nc.dram_tensor("ident_in", [128, 512], F32, kind="ExternalInput").ap()
    oneh_in = nc.dram_tensor("oneh_in", [40, 8 * 128], F32, kind="ExternalInput").ap()
    x2out = nc.dram_tensor("x2out", [D, TB], F32, kind="ExternalOutput").ap()

    dram_x1 = nc.dram_tensor("dram_x1", [128, 8 * TB], F32, kind="Internal").ap()
    dram_y = nc.dram_tensor("dram_y", [128, 8 * TB], F32, kind="Internal").ap()
    dbg_x1 = nc.dram_tensor("dbg_x1", [128, 8 * TB], F32, kind="ExternalOutput").ap()
    dbg_y = nc.dram_tensor("dbg_y", [128, 8 * TB], F32, kind="ExternalOutput").ap()
    dbg_r2 = nc.dram_tensor("dbg_r2", [128, 8 * TB], F32, kind="ExternalOutput").ap()
    dbg_gs = nc.dram_tensor("dbg_gs", [40, TB], F32, kind="ExternalOutput").ap()
    dbg_h = nc.dram_tensor("dbg_h", [128, 8 * 512], F32, kind="ExternalOutput").ap()
    dbg_x3 = nc.dram_tensor("dbg_x3", [128, 8 * 512], F32, kind="ExternalOutput").ap()
    dbg_gb = nc.dram_tensor("dbg_gb", [128, 1024], F32, kind="ExternalOutput").ap()

    TH = 512

    with ExitStack() as ctx:
        en = ctx.enter_context
        res_sb = en(nc.sbuf_tensor("res_sb", [128, 8 * TB], F32))
        stage_sb = en(nc.sbuf_tensor("stage_sb", [128, 8 * 512], F32))
        xbf = en(nc.sbuf_tensor("xbf", [128, 8 * TB], BF16))
        h_sb = en(nc.sbuf_tensor("h_sb", [128, 8 * TH], BF16))
        x3_sb = en(nc.sbuf_tensor("x3_sb", [128, 8 * TH], BF16))
        gbc_sb = en(nc.sbuf_tensor("gbc_sb", [128, TH], F32))
        selbc_sb = en(nc.sbuf_tensor("selbc_sb", [128, TH], BF16))
        u_sb = [stage_sb[:, 0:512], stage_sb[:, 512:1024]]
        t_sb = [stage_sb[:, 1024:1536], stage_sb[:, 1536:2048]]
        sl_sb = [stage_sb[:, 2048:2560], stage_sb[:, 2560:3072]]
        w1_sb = [en(nc.sbuf_tensor(f"w1_sb{i}", [128, 8 * 1024], BF16)) for i in range(2)]
        w2_sb = en(nc.sbuf_tensor("w2_sb", [128, 8 * 1024], BF16))
        w3_sb = en(nc.sbuf_tensor("w3_sb", [128, 8 * 1024], BF16))
        swc = [{1: w1_sb[0][:, 0:4096], 2: w1_sb[0][:, 4096:8192],
                3: w1_sb[1][:, 0:4096]},
               {1: w1_sb[1][:, 4096:8192], 2: w2_sb[:, 0:4096],
                3: w2_sb[:, 4096:8192]}]
        hh_sb = w3_sb[:, 0:8192]
        gatew_sb = en(nc.sbuf_tensor("gatew_sb", [128, 64], BF16))
        bias_sb = en(nc.sbuf_tensor("bias_sb", [128, 192], F32))
        ident_sb = en(nc.sbuf_tensor("ident_sb", [128, 512], F32))
        oneh_sb = en(nc.sbuf_tensor("oneh_sb", [40, 8 * 128], BF16))
        gs_pack = en(nc.sbuf_tensor("gs_pack", [40, TB], BF16))
        g_fm = gs_pack[0:8, :]
        sel_fm = gs_pack[32:40, :]
        sc = {}
        sc["xb1"] = [h_sb[:, 0:512], h_sb[:, 512:1024]]
        sc["xb2"] = [h_sb[:, 1024:1536], h_sb[:, 1536:2048]]
        sc["mean"] = stage_sb[0:1, 3072:3584]
        sc["msq"] = stage_sb[0:1, 3584:4096]
        sc["m2"] = stage_sb[0:1, 2048:2560]
        for nm in ("nmax", "rs", "rr", "den", "rden"):
            sc[nm] = en(nc.sbuf_tensor(f"rt_{nm}", [128, 1], F32))
        for nm in ("el", "scores", "t8", "sel", "g"):
            sc[nm] = en(nc.sbuf_tensor(f"rt_{nm}", [128, 8], F32))
        sc["eps"] = en(nc.sbuf_tensor("eps_sb", [1, 1], F32))
        sc["zpad"] = en(nc.sbuf_tensor("zpad_sb", [128, 512], BF16))
        sc["g_bf"] = en(nc.sbuf_tensor("g_bf", [128, 8], BF16))
        sc["sel_bf"] = en(nc.sbuf_tensor("sel_bf", [128, 8], BF16))
        sc["identb"] = en(nc.sbuf_tensor("identb_sb", [128, 512], BF16))
        sc["onescol_bf"] = en(nc.sbuf_tensor("onescol_bf", [128, 1], BF16))
        sc["onesrow"] = en(nc.sbuf_tensor("onesrow", [1, 128], F32))
        sc["ident"] = ident_sb
        sc["g_fm"] = g_fm
        sc["sel_fm"] = sel_fm

        pb = [en(nc.psum_tensor(f"pb{i}", [128, 512], F32)) for i in range(8)]
        sc["gt_ps"] = pb[6]
        sc["st_ps"] = pb[7]

        blk = en(nc.Block())
        p = Prog(nc, ctx, blk)
        T, V, A, G, SY = p.tensor, p.vector, p.scalar, p.gpsimd, p.sync

        G.dma(gatew_sb[:], gate_pk)
        SY.dma(bias_sb[:], bias_pk)
        tok_id = SY.dma(ident_sb[:], ident_in)
        G.dma(sc["identb"][:], ident_in)
        tok_g0 = G.dma(oneh_sb[:], oneh_in)
        T.wait(tok_id)
        V.op(lambda e: e.memset(sc["eps"][:], 1e-5))
        V.op(lambda e: e.memset(sc["onescol_bf"][:], 1.0))
        V.op(lambda e: e.memset(sc["zpad"][:], 0.0))
        tok_k = V.op(lambda e: e.memset(sc["onesrow"][:], 1.0), sig=True)
        T.wait(tok_g0)
        A.wait(tok_g0)

        resr = res_sb[:].rearrange("p (dc t) -> p dc t", t=TB)
        tadd = None
        for tc in range(4):
            tx = SY.dma(resr[:, :, tc * 512:(tc + 1) * 512],
                        xT.rearrange("(dc p) t -> p dc t", p=128)[:, :, tc * 512:(tc + 1) * 512])
            if tadd is not None:
                SY.wait(tadd)
            ts = SY.dma(stage_sb[:].rearrange("p (dc t) -> p dc t", t=512),
                        attnT.rearrange("(dc p) t -> p dc t", p=128)[:, :, tc * 512:(tc + 1) * 512])
            V.wait(tx)
            V.wait(ts)
            for dc in range(8):
                tadd = V.op(lambda e, dc=dc, tc=tc: e.tensor_add(
                    res_sb[:, dc * TB + tc * 512:dc * TB + (tc + 1) * 512],
                    res_sb[:, dc * TB + tc * 512:dc * TB + (tc + 1) * 512],
                    stage_sb[:, dc * 512:(dc + 1) * 512]), sig=(dc == 7))
        T.wait(tok_k)
        T.wait(tadd)
        tok_ln1 = _emit_ln_stats_norm(p, nc, res_sb, xbf, sc,
                                      [pb[0], pb[1], pb[2], pb[3]], TB)

        SY.wait(tok_ln1)
        tok_spill = SY.dma(dram_x1, res_sb[:])
        SY.dma(dbg_x1, res_sb[:])

        T.wait(tok_ln1)
        tok_rt = _emit_router(p, nc, xbf, sc, pb[5], gatew_sb, TB, renorm=False)

        G.wait(tok_rt)
        G.dma(dbg_gs, gs_pack[:])

        tok_w1 = [None, None]
        tok_w1[0] = G.dma(w1_sb[0][:].rearrange("p (dc o) -> p dc o", o=1024),
                          w1T[0].rearrange("(dc p) o -> p dc o", p=128))
        tok_w2 = G.dma(w2_sb[:].rearrange("p (dc o) -> p dc o", o=1024),
                       w2T[0].rearrange("(dc p) o -> p dc o", p=128))
        tok_w3 = G.dma(w3_sb[:].rearrange("p (dc o) -> p dc o", o=1024),
                       w3T[0].rearrange("(dc p) o -> p dc o", p=128))
        pmask = {}
        wlast = {}
        ydma = []
        A.wait(tok_ln1)
        for e_i in range(8):
            w1b = w1_sb[e_i % 2]
            if e_i + 1 < 8:
                G.wait(wlast.get(("w1", (e_i + 1) % 2)))
                tok_w1[(e_i + 1) % 2] = G.dma(
                    w1_sb[(e_i + 1) % 2][:].rearrange("p (dc o) -> p dc o", o=1024),
                    w1T[e_i + 1].rearrange("(dc p) o -> p dc o", p=128))
            for th in range(4):
                T.wait(tok_rt)
                T.wait(pmask.get(4))
                tg1 = T.op(lambda e, e_i=e_i, th=th: e.matmul(
                    pb[4][:, :], oneh_sb[0:8, e_i * 128:(e_i + 1) * 128],
                    g_fm[:, th * TH:(th + 1) * TH], start=True, stop=True), sig=True)
                T.wait(pmask.get(5))
                tg2 = T.op(lambda e, e_i=e_i, th=th: e.matmul(
                    pb[5][:, :], oneh_sb[32:40, e_i * 128:(e_i + 1) * 128],
                    sel_fm[:, th * TH:(th + 1) * TH], start=True, stop=True), sig=True)
                V.wait(tg1)
                tgb = V.op(lambda e: e.tensor_copy(gbc_sb[:, 0:TH], pb[4][:, 0:TH]),
                           sig=True)
                V.wait(tg2)
                tsb = V.op(lambda e: e.tensor_copy(selbc_sb[:, 0:TH], pb[5][:, 0:TH]),
                           sig=True)
                pmask[4] = tgb
                pmask[5] = tsb
                for which, wsb, tok_wx in ((1, w1b, tok_w1[e_i % 2]),
                                           (3, w3_sb, tok_w3)):
                    for hc in range(8):
                        bank = hc % 2
                        T.wait(tok_wx)
                        T.wait(pmask.get(bank))
                        def pmm(e, wsb=wsb, hc=hc, th=th, bank=bank):
                            for dc in range(8):
                                inst = e.matmul(
                                    pb[bank][:, :],
                                    wsb[:, dc * 1024 + hc * 128:dc * 1024 + (hc + 1) * 128],
                                    xbf[:, dc * TB + th * TH:dc * TB + (th + 1) * TH],
                                    start=(dc == 0), stop=(dc == 7))
                            return inst
                        tp = T.op(pmm, sig=True)
                        V.wait(tp)
                        V.wait(tgb)
                        if which == 1:
                            V.wait(wlast.get(("u", bank)))
                            tu = V.op(lambda e, bank=bank: e.tensor_mul(
                                u_sb[bank][:, :], pb[bank][:, :], gbc_sb[:, 0:TH]),
                                sig=True)
                            A.wait(tu)
                            A.wait(wlast.get("h_read"))
                            boff = (e_i * 3 + 0) * 8 + hc
                            tsil = A.op(lambda e, bank=bank, hc=hc, boff=boff:
                                        e.activation(h_sb[:, hc * TH:(hc + 1) * TH],
                                                     u_sb[bank][:, :], AF.Silu,
                                                     bias=bias_sb[:, boff:boff + 1]),
                                        sig=True)
                            pmask[bank] = tu
                            wlast[("u", bank)] = tsil
                        else:
                            tx3 = V.op(lambda e, bank=bank, hc=hc: e.tensor_mul(
                                x3_sb[:, hc * TH:(hc + 1) * TH], pb[bank][:, :],
                                gbc_sb[:, 0:TH]), sig=True)
                            pmask[bank] = tx3
                    wlast[("w1", e_i % 2) if which == 1 else "w3"] = tp
                for dcc in range(8):
                    bank = 2 + dcc % 2
                    T.wait(tok_w2)
                    T.wait(wlast.get(("u", 0)))
                    T.wait(wlast.get(("u", 1)))
                    T.wait(pmask.get(bank))
                    def p2mm(e, dcc=dcc):
                        for hc in range(8):
                            inst = e.matmul(
                                pb[2 + dcc % 2][:, :],
                                w2_sb[:, hc * 1024 + dcc * 128:hc * 1024 + (dcc + 1) * 128],
                                h_sb[:, hc * TH:(hc + 1) * TH],
                                start=(hc == 0), stop=(hc == 7))
                        return inst
                    tp2 = T.op(p2mm, sig=True)
                    V.wait(tp2)
                    V.wait(wlast.get(("u", bank - 2)))
                    b2off = (e_i * 3 + 1) * 8 + dcc
                    tu2 = V.op(lambda e, bank=bank, b2off=b2off: e.scalar_tensor_tensor(
                        u_sb[bank - 2][:, :], pb[bank][:, :], bias_sb[:, b2off:b2off + 1],
                        selbc_sb[:, 0:TH], ALU.add, ALU.mult), sig=True)
                    pmask[bank] = tu2
                    wlast[("u", bank - 2)] = tu2
                    b3off = (e_i * 3 + 2) * 8 + dcc
                    V.wait(wlast.get(("tdma", bank - 2)))
                    tprod = V.op(lambda e, bank=bank, dcc=dcc, b3off=b3off:
                                 e.scalar_tensor_tensor(
                                     t_sb[bank - 2][:, :],
                                     x3_sb[:, dcc * TH:(dcc + 1) * TH],
                                     bias_sb[:, b3off:b3off + 1],
                                     u_sb[bank - 2][:, :], ALU.add, ALU.mult),
                                 sig=True)
                    G.wait(tprod)
                    G.wait(wlast.get(("yreg", th, dcc)))
                    tok_y = G.dma_accum(
                        dram_y[:, dcc * TB + th * TH:dcc * TB + (th + 1) * TH],
                        t_sb[bank - 2][:, :], first=(e_i == 0))
                    ydma.append(tok_y)
                    wlast[("tdma", bank - 2)] = tok_y
                    wlast[("yreg", th, dcc)] = tok_y
                    wlast["h_read"] = tp2
            if e_i + 1 < 8:
                G.wait(wlast.get("w3"))
                tok_w3 = G.dma(w3_sb[:].rearrange("p (dc o) -> p dc o", o=1024),
                               w3T[e_i + 1].rearrange("(dc p) o -> p dc o", p=128))
                G.wait(wlast.get("h_read"))
                tok_w2 = G.dma(w2_sb[:].rearrange("p (dc o) -> p dc o", o=1024),
                               w2T[e_i + 1].rearrange("(dc p) o -> p dc o", p=128))
        tok_ylast = ydma[-1]
        G.wait(tok_ylast)
        G.dma(dbg_y, dram_y)
        G.dma(dbg_h, h_sb[:])
        G.dma(dbg_x3, x3_sb[:])
        G.dma(dbg_gb[:, 0:512], gbc_sb[:])
        G.dma(dbg_gb[:, 512:1024], selbc_sb[:])

        SY.wait(tok_spill)
        tln = None
        for tc in range(4):
            tx = SY.dma(resr[:, :, tc * 512:(tc + 1) * 512],
                        dram_x1.rearrange("p (dc t) -> p dc t", t=TB)[:, :, tc * 512:(tc + 1) * 512])
            if tln is not None:
                G.wait(tln)
            G.wait(tok_ylast)
            ty = G.dma(stage_sb[:].rearrange("p (dc t) -> p dc t", t=512),
                       dram_y.rearrange("p (dc t) -> p dc t", t=TB)[:, :, tc * 512:(tc + 1) * 512])
            V.wait(tx)
            V.wait(ty)
            for dc in range(8):
                tln = V.op(lambda e, dc=dc, tc=tc: e.tensor_add(
                    res_sb[:, dc * TB + tc * 512:dc * TB + (tc + 1) * 512],
                    res_sb[:, dc * TB + tc * 512:dc * TB + (tc + 1) * 512],
                    stage_sb[:, dc * 512:(dc + 1) * 512]), sig=(dc == 7))
        G.wait(wlast.get(("w1", 0)))
        G.wait(wlast.get(("w1", 1)))
        G.wait(wlast.get("w3"))
        G.wait(wlast.get("h_read"))
        tok_sw = [None, None]
        def load_sw(i, ic):
            G.dma(swc[i][1][:].rearrange("p (dc t) -> p dc t", t=512),
                  sw1T.rearrange("(dc p) o -> p dc o", p=128)[:, :, ic * 512:(ic + 1) * 512])
            G.dma(swc[i][2][:].rearrange("p (dc t) -> p dc t", t=512),
                  sw2T.rearrange("(dc p) o -> p dc o", p=128)[:, :, ic * 512:(ic + 1) * 512])
            return G.dma(swc[i][3][:].rearrange("p (ic o) -> p ic o", o=1024),
                         sw3T.rearrange("(icc p) o -> p icc o", p=128)[:, ic * 4:(ic + 1) * 4, :])
        tok_sw[0] = load_sw(0, 0)
        swlast = {}
        tza = None
        for ic in range(4):
            i = ic % 2
            if ic + 1 < 4:
                G.wait(swlast.get((ic + 1) % 2))
                tok_sw[(ic + 1) % 2] = load_sw((ic + 1) % 2, ic + 1)
            for isc in range(4):
                for tc in range(4):
                    b1k, b2k = isc % 2, 2 + isc % 2
                    T.wait(tok_sw[i])
                    T.wait(pmask.get(("sh", b1k)))
                    def h1mm(e, i=i, isc=isc, tc=tc, b1k=b1k):
                        for dc in range(8):
                            inst = e.matmul(
                                pb[b1k][:, :],
                                swc[i][1][:, dc * 512 + isc * 128:dc * 512 + (isc + 1) * 128],
                                xbf[:, dc * TB + tc * 512:dc * TB + (tc + 1) * 512],
                                start=(dc == 0), stop=(dc == 7))
                        return inst
                    th1 = T.op(h1mm, sig=True)
                    T.wait(pmask.get(("sh", b2k)))
                    def h2mm(e, i=i, isc=isc, tc=tc, b2k=b2k):
                        for dc in range(8):
                            inst = e.matmul(
                                pb[b2k][:, :],
                                swc[i][2][:, dc * 512 + isc * 128:dc * 512 + (isc + 1) * 128],
                                xbf[:, dc * TB + tc * 512:dc * TB + (tc + 1) * 512],
                                start=(dc == 0), stop=(dc == 7))
                        return inst
                    th2 = T.op(h2mm, sig=True)
                    A.wait(th1)
                    A.wait(swlast.get(("sl", isc % 2)))
                    tsl = A.op(lambda e, b1k=b1k, isc=isc: e.activation(
                        sl_sb[isc % 2][:, :], pb[b1k][:, :], AF.Silu), sig=True)
                    pmask[("sh", b1k)] = tsl
                    V.wait(tsl)
                    V.wait(th2)
                    V.wait(swlast.get("ztok"))
                    V.wait(wlast.get("w3"))
                    thh = V.op(lambda e, b2k=b2k, isc=isc, tc=tc: e.tensor_mul(
                        hh_sb[:, isc * TB + tc * 512:isc * TB + (tc + 1) * 512],
                        pb[b2k][:, :], sl_sb[isc % 2][:, :]), sig=True)
                    pmask[("sh", b2k)] = thh
                    swlast[("sl", isc % 2)] = thh
                    swlast[("hh", isc, tc)] = thh
            for tc in range(4):
                for dc in range(8):
                    bz = 4 + dc % 2
                    T.wait(pmask.get(("sh", bz)))
                    for isc in range(4):
                        T.wait(swlast.get(("hh", isc, tc)))
                    def zmm(e, i=i, tc=tc, dc=dc, bz=bz):
                        for isc in range(4):
                            inst = e.matmul(
                                pb[bz][:, :],
                                swc[i][3][:, isc * 1024 + dc * 128:isc * 1024 + (dc + 1) * 128],
                                hh_sb[:, isc * TB + tc * 512:isc * TB + (tc + 1) * 512],
                                start=(isc == 0), stop=(isc == 3))
                        return inst
                    tz = T.op(zmm, sig=True)
                    V.wait(tz)
                    V.wait(tln)
                    tza = V.op(lambda e, dc=dc, tc=tc, bz=bz: e.tensor_add(
                        res_sb[:, dc * TB + tc * 512:dc * TB + (tc + 1) * 512],
                        res_sb[:, dc * TB + tc * 512:dc * TB + (tc + 1) * 512],
                        pb[bz][:, :]), sig=True)
                    pmask[("sh", bz)] = tza
            swlast[i] = tz
            swlast["ztok"] = tz
        tok_z = tza

        SY.wait(tok_z)
        tok_dbg2 = SY.dma(dbg_r2, res_sb[:])
        V.wait(tok_dbg2)
        T.wait(tok_z)
        V.wait(wlast.get("h_read"))
        tok_ln2 = _emit_ln_stats_norm(p, nc, res_sb, xbf, sc,
                                      [pb[0], pb[1], pb[2], pb[3]], TB)
        SY.wait(tok_ln2)
        tout = SY.dma(x2out.rearrange("(dc p) t -> p dc t", p=128), resr)
        SY.wait(tout)
        p.flush()
    return nc






def build_launch_C():
    from contextlib import ExitStack
    nc = bass.Bass("TRN2", target_bir_lowering=False, debug=False,
                   detect_race_conditions=False)
    x2T = nc.dram_tensor("x2T", [D, S], F32, kind="ExternalInput").ap()
    wqTh = nc.dram_tensor("wqTh", [D, 256], F32, kind="ExternalInput").ap()
    wkTh = nc.dram_tensor("wkTh", [D, 256], F32, kind="ExternalInput").ap()
    wvTh = nc.dram_tensor("wvTh", [D, 256], F32, kind="ExternalInput").ap()
    cos2 = nc.dram_tensor("cos2", [128, S], F32, kind="ExternalInput").ap()
    sin2 = nc.dram_tensor("sin2", [128, S], F32, kind="ExternalInput").ap()
    r128T = nc.dram_tensor("r128T", [128, 128], F32, kind="ExternalInput").ap()
    maskc = nc.dram_tensor("maskc", [128, 4 * 512], F32, kind="ExternalInput").ap()
    attn1 = nc.dram_tensor("attn1", [256, S], F32, kind="ExternalOutput").ap()

    NT = S // 512
    NKC = S // 128

    with ExitStack() as ctx:
        en = ctx.enter_context
        wq_sb = en(nc.sbuf_tensor("wq_sb", [128, 8 * 256], BF16))
        wk_sb = en(nc.sbuf_tensor("wk_sb", [128, 8 * 256], BF16))
        wv_sb = en(nc.sbuf_tensor("wv_sb", [128, 8 * 256], BF16))
        x_sb = [en(nc.sbuf_tensor(f"x_sb{i}", [128, 8 * 512], BF16)) for i in range(2)]
        q2_sb = [en(nc.sbuf_tensor(f"q2_sb{i}", [128, S], BF16)) for i in range(2)]
        k2_sb = [en(nc.sbuf_tensor(f"k2_sb{i}", [128, S], BF16)) for i in range(2)]
        v_sb = en(nc.sbuf_tensor("v_sb", [128, NKC * 4 * 65], BF16))
        cos_sb = en(nc.sbuf_tensor("cos_sb", [128, S], BF16))
        sin_sb = en(nc.sbuf_tensor("sin_sb", [128, S], BF16))
        r_sb = en(nc.sbuf_tensor("r_sb", [128, 128], BF16))
        mask_sb = en(nc.sbuf_tensor("mask_sb", [128, 4 * 512], F32))
        qraw = [en(nc.sbuf_tensor(f"qraw{i}", [128, 512], BF16)) for i in range(2)]
        t1_sb = [en(nc.sbuf_tensor(f"t1_sb{i}", [128, 512], F32)) for i in range(2)]
        t2_sb = [en(nc.sbuf_tensor(f"t2_sb{i}", [128, 512], F32)) for i in range(2)]
        expst8 = en(nc.sbuf_tensor("expst8", [128, 16 * 512], BF16))
        expst = [expst8[:, i * 512:(i + 1) * 512] for i in range(16)]
        bc_sb = [en(nc.sbuf_tensor(f"bc_sb{i}", [64, 512], F32)) for i in range(2)]
        rd_sb = [en(nc.sbuf_tensor(f"rd_sb{i}", [1, 512], F32)) for i in range(2)]
        out_sb = [en(nc.sbuf_tensor(f"out_sb{i}", [64, 512], F32)) for i in range(2)]
        ones_sb = en(nc.sbuf_tensor("ones_sb", [1, 64], F32))
        ebias_sb = en(nc.sbuf_tensor("ebias_sb", [128, 1], F32))
        zpad_sb = en(nc.sbuf_tensor("zpad_sbC", [128, 512], BF16))

        pb = [en(nc.psum_tensor(f"pb{i}", [128, 512], F32)) for i in range(8)]

        blk = en(nc.Block())
        p = Prog(nc, ctx, blk)
        T, V, A, G, SY = p.tensor, p.vector, p.scalar, p.gpsimd, p.sync

        G.dma(wq_sb[:].rearrange("p (dc o) -> p dc o", o=256),
              wqTh.rearrange("(dc p) o -> p dc o", p=128))
        G.dma(wk_sb[:].rearrange("p (dc o) -> p dc o", o=256),
              wkTh.rearrange("(dc p) o -> p dc o", p=128))
        G.dma(wv_sb[:].rearrange("p (dc o) -> p dc o", o=256),
              wvTh.rearrange("(dc p) o -> p dc o", p=128))
        tok_w = G.dma(r_sb[:], r128T)
        G.dma(cos_sb[:], cos2)
        G.dma(sin_sb[:], sin2)
        tok_c = SY.dma(mask_sb[:], maskc)
        tok_cs = G.tok()
        V.wait(tok_cs)
        V.op(lambda e: e.memset(v_sb[:], 0.0))
        V.op(lambda e: e.memset(
            v_sb[:].rearrange("p (kc w) -> p kc w", w=65)[:, :, 64:65], 1.0))
        V.op(lambda e: e.memset(ebias_sb[:], EXP_BIAS))
        V.op(lambda e: e.memset(zpad_sb[:], 0.0))
        tok_ones = V.op(lambda e: e.memset(ones_sb[:], 1.0), sig=True)
        V.wait(tok_c)

        evac = {}
        for tc in range(NT):
            xb = x_sb[tc % 2]
            if tc >= 2:
                G.wait(evac.get(("x", tc % 2)))
            tok_x = G.dma(
                xb[:].rearrange("p (dc t) -> p dc t", t=512),
                x2T.rearrange("(dc p) t -> p dc t", p=128)[:, :, tc * 512:(tc + 1) * 512])
            T.wait(tok_x)
            T.wait(tok_w)
            T.wait(tok_ones)
            for which, wsb, dst in (("q", wq_sb, q2_sb), ("k", wk_sb, k2_sb)):
                for oc in range(2):
                    bank = 0 if which == "q" else 1
                    T.wait(evac.get((which, oc)))
                    def prj(e, wsb=wsb, oc=oc, xb=xb, bank=bank):
                        for dc in range(8):
                            inst = e.matmul(
                                pb[bank][:, :],
                                wsb[:, dc * 256 + oc * 128:dc * 256 + (oc + 1) * 128],
                                xb[:, dc * 512:(dc + 1) * 512],
                                start=(dc == 0), stop=(dc == 7))
                        return inst
                    tp = T.op(prj, sig=True)
                    V.wait(tp)
                    V.wait(evac.get((which, oc, "qraw")))
                    tqr = V.op(lambda e, oc=oc, bank=bank: e.tensor_copy(
                        qraw[oc][:, :], pb[bank][:, :]), sig=True)
                    T.wait(tqr)
                    T.wait(evac.get((which, oc, "rot")))
                    trm = T.op(lambda e, oc=oc: e.matmul(
                        pb[2 + oc][:, :], r_sb[:, :], qraw[oc][:, :],
                        start=True, stop=True), sig=True)
                    V.op(lambda e, oc=oc, tc=tc: e.tensor_mul(
                        t1_sb[oc][:, :], qraw[oc][:, :],
                        cos_sb[:, tc * 512:(tc + 1) * 512]))
                    V.wait(trm)
                    V.op(lambda e, oc=oc, tc=tc: e.tensor_mul(
                        t2_sb[oc][:, :], pb[2 + oc][:, :],
                        sin_sb[:, tc * 512:(tc + 1) * 512]))
                    tev = V.op(lambda e, oc=oc, tc=tc, dst=dst: e.tensor_add(
                        dst[oc][:, tc * 512:(tc + 1) * 512],
                        t1_sb[oc][:, :], t2_sb[oc][:, :]), sig=True)
                    evac[(which, oc)] = tev
                    evac[(which, oc, "qraw")] = trm
                    evac[(which, oc, "rot")] = tev
            for ts in range(4):
                kt = tc * 4 + ts
                T.wait(evac.get("v"))
                def vmm(e, xb=xb, ts=ts):
                    for dc in range(8):
                        e.matmul(pb[4][:, 0:256],
                                 xb[:, dc * 512 + ts * 128:dc * 512 + (ts + 1) * 128],
                                 wv_sb[:, dc * 256:(dc + 1) * 256],
                                 start=(dc == 0), stop=False)
                    return e.matmul(pb[4][:, 0:512],
                                    xb[:, ts * 128:(ts + 1) * 128],
                                    zpad_sb[:, 0:512], start=False, stop=True)
                tv = T.op(vmm, sig=True)
                V.wait(tv)
                tvv = None
                for h in range(4):
                    tvv = V.op(lambda e, kt=kt, h=h: e.tensor_copy(
                        v_sb[:, (kt * 4 + h) * 65:(kt * 4 + h) * 65 + 64],
                        pb[4][:, h * 64:(h + 1) * 64]), sig=(h == 3))
                evac["v"] = tvv
            evac[("x", tc % 2)] = tv

        exp_tok = {}
        pv_tok = {}
        st_tok = {}
        last_pv_use = {}
        last_bc_use = {}
        last_rd_use = {}
        store_tok = {}
        jobs = []
        for h in range(4):
            for qc in range(NT):
                jobs.append((h, qc))
        steps = []
        for j, (h, qc) in enumerate(jobs):
            for kc in range(4 * qc + 4):
                steps.append((j, h, qc, kc))

        def emit_st(gidx, h, qc, kc):
            stb = pb[gidx % 2]
            T.wait(exp_tok.get(gidx - 2))
            hp = h % 2
            ksb = k2_sb[h // 2]
            qsb = q2_sb[h // 2]
            return T.op(lambda e: e.matmul(
                stb[:, :],
                ksb[64 * hp:64 * hp + 64, kc * 128:(kc + 1) * 128],
                qsb[64 * hp:64 * hp + 64, qc * 512:(qc + 1) * 512],
                start=True, stop=True), sig=True)

        tok_p1q = evac[("q", 1)]
        tok_p1k = evac[("k", 1)]
        tok_p1v = evac["v"]
        GRP = 16
        assert len(steps) % GRP == 0
        ngrp = len(steps) // GRP
        for b in range(ngrp + 1):
            if b >= 1:
              for gg in range((b - 1) * GRP, b * GRP):
                j, h, qc, kc = steps[gg]
                pvb = pb[2 + (j % 2)]
                T.wait(exp_tok[gg])
                if kc == 0:
                    T.wait(last_pv_use.get(j % 2))
                pv_tok[gg] = T.op(lambda e, pvb=pvb, h=h, kc=kc, gg=gg: e.matmul(
                    pvb[0:65, :],
                    v_sb[:, (kc * 4 + h) * 65:(kc * 4 + h) * 65 + 65],
                    expst[gg % GRP][:, :],
                    start=(kc == 0), stop=False), sig=True)
                if kc == 4 * qc + 3:
                    tz = T.op(lambda e, pvb=pvb: e.matmul(
                        pvb[0:65, 0:512], v_sb[:, 0:65], zpad_sb[:, 0:512],
                        start=False, stop=True), sig=True)
                    pv_tok[gg] = tz
                    V.wait(tz)
                    V.wait(last_rd_use.get(j % 2))
                    trd = V.op(lambda e, j=j, pvb=pvb: e.reciprocal(
                        rd_sb[j % 2][:, :], pvb[64:65, :]), sig=True)
                    T.wait(trd)
                    T.wait(last_bc_use.get(j % 2))
                    bcb = pb[4 + (j % 2)]
                    tbc = T.op(lambda e, bcb=bcb, j=j: e.matmul(
                        bcb[0:64, :], ones_sb[0:1, 0:64], rd_sb[j % 2][:, :],
                        start=True, stop=True), sig=True)
                    last_rd_use[j % 2] = tbc
                    V.wait(tbc)
                    tcp = V.op(lambda e, bcb=bcb, j=j: e.tensor_copy(
                        bc_sb[j % 2][:, :], bcb[0:64, :]), sig=True)
                    last_bc_use[j % 2] = tcp
                    V.wait(store_tok.get(j % 2))
                    tnorm = V.op(lambda e, j=j, pvb=pvb: e.tensor_mul(
                        out_sb[j % 2][:, :], pvb[0:64, :], bc_sb[j % 2][:, :]),
                        sig=True)
                    last_pv_use[j % 2] = tnorm
                    SY.wait(tnorm)
                    store_tok[j % 2] = SY.dma(
                        attn1[64 * h:64 * (h + 1), qc * 512:(qc + 1) * 512],
                        out_sb[j % 2][:, :])
            if b < ngrp:
                for g in range(b * GRP, (b + 1) * GRP):
                    j, h, qc, kc = steps[g]
                    if g == 0:
                        T.wait(tok_p1q)
                        T.wait(tok_p1k)
                        T.wait(tok_p1v)
                        st_tok[0] = emit_st(0, h, qc, kc)
                    stb = pb[g % 2]
                    A.wait(st_tok[g])
                    A.wait(pv_tok.get(g - GRP))
                    exp_tok[g] = A.op(lambda e, stb=stb, g=g: e.activation(
                        expst[g % GRP][:, :], stb[:, :], AF.Exp,
                        bias=ebias_sb[:, 0:1], scale=0.125), sig=True)
                    if g + 1 < len(steps):
                        jn, hn, qcn, kcn = steps[g + 1]
                        st_tok[g + 1] = emit_st(g + 1, hn, qcn, kcn)
                    m = kc - 4 * qc
                    if m >= 0:
                        V.wait(exp_tok[g])
                        exp_tok[g] = V.op(lambda e, g=g, m=m: e.tensor_mul(
                            expst[g % GRP][:, :], expst[g % GRP][:, :],
                            mask_sb[:, m * 512:(m + 1) * 512]), sig=True)
        SY.wait(store_tok.get(0))
        SY.wait(store_tok.get(1))
        p.flush()
    return nc







def build_launch_D():
    from contextlib import ExitStack
    nc = bass.Bass("TRN2", target_bir_lowering=False, debug=False,
                   detect_race_conditions=False)
    x2Ts = nc.dram_tensor("x2Ts", [D, TB], F32, kind="ExternalInput").ap()
    attn1Ts = nc.dram_tensor("attn1Ts", [D, TB], F32, kind="ExternalInput").ap()
    woT = nc.dram_tensor("woT", [D, D], F32, kind="ExternalInput").ap()
    sw1 = nc.dram_tensor("sw1", [D, 2048], F32, kind="ExternalInput").ap()
    sw2 = nc.dram_tensor("sw2", [D, 2048], F32, kind="ExternalInput").ap()
    sw3 = nc.dram_tensor("sw3", [2048, D], F32, kind="ExternalInput").ap()
    ew1 = nc.dram_tensor("ew1", [8, D, 2048], F32, kind="ExternalInput").ap()
    ew2 = nc.dram_tensor("ew2", [8, D, 2048], F32, kind="ExternalInput").ap()
    ew3 = nc.dram_tensor("ew3", [8, 2048, D], F32, kind="ExternalInput").ap()
    rw_pk = nc.dram_tensor("rw_pk", [128, 64], F32, kind="ExternalInput").ap()
    rb_in = nc.dram_tensor("rb_in", [1, 8], F32, kind="ExternalInput").ap()
    ident_in = nc.dram_tensor("ident_in", [128, 512], F32, kind="ExternalInput").ap()
    oneh_in = nc.dram_tensor("oneh_in", [40, 8 * 128], F32, kind="ExternalInput").ap()
    outfm = nc.dram_tensor("outfm", [D, TB], F32, kind="ExternalOutput").ap()

    dram_x3 = nc.dram_tensor("dram_x3", [128, 8 * TB], F32, kind="Internal").ap()
    dram_r = nc.dram_tensor("dram_r", [128, 8 * TB], F32, kind="Internal").ap()

    TH = 512

    with ExitStack() as ctx:
        en = ctx.enter_context
        res_sb = en(nc.sbuf_tensor("res_sb", [128, 8 * TB], F32))
        stage_sb = en(nc.sbuf_tensor("stage_sb", [128, 8 * 512], F32))
        xbf = en(nc.sbuf_tensor("xbf", [128, 8 * TB], BF16))
        wa = [en(nc.sbuf_tensor(f"wa{i}", [128, 8 * 1024], BF16)) for i in range(2)]
        wb = [en(nc.sbuf_tensor(f"wb{i}", [128, 8 * 1024], BF16)) for i in range(2)]
        wc = en(nc.sbuf_tensor("wc0", [128, 8 * 1024], BF16))
        h_half = res_sb[:, 0:2048].bitcast(BF16)
        gbc_sb = res_sb[:, 2048:2560]
        u_sb = [stage_sb[:, 0:512], stage_sb[:, 512:1024]]
        t_sb = [stage_sb[:, 1024:1536], stage_sb[:, 1536:2048]]
        sl_sb = [stage_sb[:, 2048:2560], stage_sb[:, 2560:3072]]
        rww_sb = en(nc.sbuf_tensor("rww_sb", [128, 64], BF16))
        rb_sb = en(nc.sbuf_tensor("rb_sb", [1, 8], BF16))
        oneh_sb = en(nc.sbuf_tensor("oneh_sb", [40, 8 * 128], BF16))
        gs_pack = en(nc.sbuf_tensor("gs_pack", [40, TB], BF16))
        g_fm = gs_pack[0:8, :]
        sel_fm = gs_pack[32:40, :]
        sc = {}
        sc["xb1"] = [en(nc.sbuf_tensor(f"lnxb1_{i}", [128, 512], BF16)) for i in range(2)]
        sc["xb2"] = [en(nc.sbuf_tensor(f"lnxb2_{i}", [128, 512], BF16)) for i in range(2)]
        sc["mean"] = stage_sb[0:1, 3072:3584]
        sc["msq"] = stage_sb[0:1, 3584:4096]
        sc["m2"] = stage_sb[0:1, 2560:3072]
        for nm in ("nmax", "rs", "rr", "den", "rden"):
            sc[nm] = en(nc.sbuf_tensor(f"rt_{nm}", [128, 1], F32))
        for nm in ("el", "scores", "t8", "sel", "g"):
            sc[nm] = en(nc.sbuf_tensor(f"rt_{nm}", [128, 8], F32))
        sc["eps"] = en(nc.sbuf_tensor("eps_sb", [1, 1], F32))
        sc["zpad"] = en(nc.sbuf_tensor("zpad_sb", [128, 512], BF16))
        sc["g_bf"] = en(nc.sbuf_tensor("g_bf", [128, 8], BF16))
        sc["sel_bf"] = en(nc.sbuf_tensor("sel_bf", [128, 8], BF16))
        sc["identb"] = en(nc.sbuf_tensor("identb_sb", [128, 512], BF16))
        sc["onescol_bf"] = en(nc.sbuf_tensor("onescol_bf", [128, 1], BF16))
        sc["onesrow"] = en(nc.sbuf_tensor("onesrow", [1, 128], F32))
        sc["onesrow_bf"] = en(nc.sbuf_tensor("onesrow_bf", [1, 128], BF16))
        sc["ident"] = None
        sc["g_fm"] = g_fm
        sc["sel_fm"] = sel_fm

        pb = [en(nc.psum_tensor(f"pb{i}", [128, 512], F32)) for i in range(8)]
        sc["gt_ps"] = pb[6]
        sc["st_ps"] = pb[7]

        blk = en(nc.Block())
        p = Prog(nc, ctx, blk)
        T, V, A, G, SY = p.tensor, p.vector, p.scalar, p.gpsimd, p.sync

        G.dma(rww_sb[:], rw_pk)
        G.dma(rb_sb[:], rb_in)
        G.dma(sc["identb"][:], ident_in)
        tok_g0 = G.dma(oneh_sb[:], oneh_in)
        V.op(lambda e: e.memset(sc["eps"][:], 1e-5))
        V.op(lambda e: e.memset(sc["onescol_bf"][:], 1.0))
        V.op(lambda e: e.memset(sc["zpad"][:], 0.0))
        V.op(lambda e: e.memset(sc["onesrow_bf"][:], 1.0))
        tok_k = V.op(lambda e: e.memset(sc["onesrow"][:], 1.0), sig=True)
        T.wait(tok_g0)
        A.wait(tok_g0)

        resr = res_sb[:].rearrange("p (dc t) -> p dc t", t=TB)
        tok_wo = G.dma(wa[0][:].rearrange("p (dc o) -> p dc o", o=1024),
                       woT.rearrange("(dc p) o -> p dc o", p=128))
        tok_ab = G.dma(xbf[:].rearrange("p (dc t) -> p dc t", t=TB),
                       attn1Ts.rearrange("(dc p) t -> p dc t", p=128))
        tok_x2 = SY.dma(resr, x2Ts.rearrange("(dc p) t -> p dc t", p=128))
        pmask = {}
        tadd = None
        for tc in range(4):
            for dc in range(8):
                bank = dc % 2
                T.wait(tok_wo)
                T.wait(tok_ab)
                T.wait(pmask.get(("wo", bank)))
                def womm(e, dc=dc, tc=tc, bank=bank):
                    for ac in range(8):
                        inst = e.matmul(
                            pb[bank][:, :],
                            wa[0][:, ac * 1024 + dc * 128:ac * 1024 + (dc + 1) * 128],
                            xbf[:, ac * TB + tc * 512:ac * TB + (tc + 1) * 512],
                            start=(ac == 0), stop=(ac == 7))
                    return inst
                tw = T.op(womm, sig=True)
                V.wait(tw)
                V.wait(tok_x2)
                tadd = V.op(lambda e, dc=dc, tc=tc, bank=bank: e.tensor_add(
                    res_sb[:, dc * TB + tc * 512:dc * TB + (tc + 1) * 512],
                    res_sb[:, dc * TB + tc * 512:dc * TB + (tc + 1) * 512],
                    pb[bank][:, :]), sig=True)
                pmask[("wo", bank)] = tadd
        T.wait(tok_k)
        T.wait(tadd)
        tok_ln3 = _emit_ln_stats_norm(p, nc, res_sb, xbf, sc,
                                      [pb[0], pb[1], pb[2], pb[3]], TB)
        SY.wait(tok_ln3)
        tok_spill = SY.dma(dram_x3, res_sb[:])

        def rb_mm(e, logit_ps):
            return e.matmul(logit_ps[:, 0:8], sc["onesrow_bf"][0:1, 0:128],
                            rb_sb[0:1, 0:8], start=False, stop=False)
        T.wait(tok_ln3)
        tok_rt = _emit_router(p, nc, xbf, sc, pb[5], rww_sb, TB, renorm=True,
                              extra_bias_mm=rb_mm)

        def load_ab(buf, e_i, half):
            G.dma(wa[buf][:].rearrange("p (dc o) -> p dc o", o=1024),
                  ew1[e_i].rearrange("(dc p) o -> p dc o", p=128)[:, :, half * 1024:(half + 1) * 1024])
            return G.dma(wb[buf][:].rearrange("p (dc o) -> p dc o", o=1024),
                         ew2[e_i].rearrange("(dc p) o -> p dc o", p=128)[:, :, half * 1024:(half + 1) * 1024])

        def load_c(e_i, half):
            return G.dma(wc[:].rearrange("p (hc o) -> p hc o", o=1024),
                         ew3[e_i].rearrange("(hcc p) o -> p hcc o", p=128)[:, half * 8:(half + 1) * 8, :])
        G.wait(tw)
        wlast = {}
        tok_we = {}
        tok_we[0] = load_ab(0, 0, 0)
        tok_wc = load_c(0, 0)
        ydma = []
        hseq = [(e_i, half) for e_i in range(8) for half in range(2)]
        for si, (e_i, half) in enumerate(hseq):
            buf = si % 2
            if si > 0:
                G.wait(wlast.get("wc"))
                tok_wc = load_c(e_i, half)
            if si + 1 < len(hseq):
                G.wait(wlast.get(("w", (si + 1) % 2)))
                tok_we[(si + 1) % 2] = load_ab((si + 1) % 2,
                                               hseq[si + 1][0], hseq[si + 1][1])
            for th in range(4):
                T.wait(tok_rt)
                T.wait(pmask.get(("b", 6)))
                tg1 = T.op(lambda e, e_i=e_i, th=th: e.matmul(
                    pb[6][:, :], oneh_sb[0:8, e_i * 128:(e_i + 1) * 128],
                    g_fm[:, th * TH:(th + 1) * TH], start=True, stop=True), sig=True)
                V.wait(tg1)
                V.wait(wlast.get("gbc_r"))
                tgb = V.op(lambda e: e.tensor_copy(gbc_sb[:, 0:TH], pb[6][:, 0:TH]),
                           sig=True)
                pmask[("b", 6)] = tgb
                for hc in range(8):
                    b1k, b2k = hc % 2, 2 + hc % 2
                    T.wait(tok_we[buf])
                    T.wait(pmask.get(("p1", b1k)))
                    def p1mm(e, buf=buf, hc=hc, th=th, b1k=b1k):
                        for dc in range(8):
                            inst = e.matmul(
                                pb[b1k][:, :],
                                wa[buf][:, dc * 1024 + hc * 128:dc * 1024 + (hc + 1) * 128],
                                xbf[:, dc * TB + th * TH:dc * TB + (th + 1) * TH],
                                start=(dc == 0), stop=(dc == 7))
                        return inst
                    tp1 = T.op(p1mm, sig=True)
                    T.wait(pmask.get(("p2", b2k)))
                    def p2mm(e, buf=buf, hc=hc, th=th, b2k=b2k):
                        for dc in range(8):
                            inst = e.matmul(
                                pb[b2k][:, :],
                                wb[buf][:, dc * 1024 + hc * 128:dc * 1024 + (hc + 1) * 128],
                                xbf[:, dc * TB + th * TH:dc * TB + (th + 1) * TH],
                                start=(dc == 0), stop=(dc == 7))
                        return inst
                    tp2 = T.op(p2mm, sig=True)
                    A.wait(tp1)
                    A.wait(wlast.get(("sl", hc % 2)))
                    tsl = A.op(lambda e, b1k=b1k, hc=hc: e.activation(
                        sl_sb[hc % 2][:, :], pb[b1k][:, :], AF.Silu), sig=True)
                    pmask[("p1", b1k)] = tsl
                    V.wait(tsl)
                    V.wait(tp2)
                    V.wait(wlast.get("h_read"))
                    thh = V.op(lambda e, b2k=b2k, hc=hc: e.tensor_mul(
                        h_half[:, hc * TH:(hc + 1) * TH], pb[b2k][:, :],
                        sl_sb[hc % 2][:, :]), sig=True)
                    pmask[("p2", b2k)] = thh
                    wlast[("sl", hc % 2)] = thh
                    wlast[("hhw", hc)] = thh
                for dcc in range(8):
                    bank = 4 + dcc % 2
                    T.wait(pmask.get(("b", bank)))
                    for hc in range(8):
                        T.wait(wlast.get(("hhw", hc)))
                    T.wait(tok_wc)
                    def eomm(e, dcc=dcc):
                        for hc in range(8):
                            inst = e.matmul(
                                pb[4 + dcc % 2][:, :],
                                wc[:, hc * 1024 + dcc * 128:hc * 1024 + (dcc + 1) * 128],
                                h_half[:, hc * TH:(hc + 1) * TH],
                                start=(hc == 0), stop=(hc == 7))
                        return inst
                    teo = T.op(eomm, sig=True)
                    V.wait(teo)
                    V.wait(tgb)
                    V.wait(wlast.get(("tdma", dcc % 2)))
                    tpr = V.op(lambda e, bank=bank, dcc=dcc: e.tensor_mul(
                        t_sb[dcc % 2][:, :], pb[bank][:, :], gbc_sb[:, 0:TH]),
                        sig=True)
                    pmask[("b", bank)] = tpr
                    G.wait(tpr)
                    G.wait(wlast.get(("yreg", th, dcc)))
                    tok_y = G.dma_accum(
                        dram_r[:, dcc * TB + th * TH:dcc * TB + (th + 1) * TH],
                        t_sb[dcc % 2][:, :], first=(si == 0))
                    ydma.append(tok_y)
                    wlast[("tdma", dcc % 2)] = tok_y
                    wlast[("yreg", th, dcc)] = tok_y
                    wlast["h_read"] = teo
                wlast["gbc_r"] = tpr
            wlast[("w", buf)] = teo
            wlast["wc"] = teo
        tok_ylast = ydma[-1]

        SY.wait(tok_spill)
        tln = None
        for tc in range(4):
            if tln is not None:
                SY.wait(tln)
            SY.wait(tok_ylast)
            tx = SY.dma(resr[:, :, tc * 512:(tc + 1) * 512],
                        dram_x3.rearrange("p (dc t) -> p dc t", t=TB)[:, :, tc * 512:(tc + 1) * 512])
            if tln is not None:
                G.wait(tln)
            G.wait(tok_ylast)
            ty = G.dma(stage_sb[:].rearrange("p (dc t) -> p dc t", t=512),
                       dram_r.rearrange("p (dc t) -> p dc t", t=TB)[:, :, tc * 512:(tc + 1) * 512])
            V.wait(tx)
            V.wait(ty)
            for dc in range(8):
                tln = V.op(lambda e, dc=dc, tc=tc: e.tensor_add(
                    res_sb[:, dc * TB + tc * 512:dc * TB + (tc + 1) * 512],
                    res_sb[:, dc * TB + tc * 512:dc * TB + (tc + 1) * 512],
                    stage_sb[:, dc * 512:(dc + 1) * 512]), sig=(dc == 7))
        swc = [{1: wa[0][:, 0:4096], 2: wa[0][:, 4096:8192], 3: wa[1][:, 0:4096]},
               {1: wa[1][:, 4096:8192], 2: wb[0][:, 0:4096], 3: wb[0][:, 4096:8192]}]
        hh_sb = wb[1][:, 0:8192]
        G.wait(wlast[("w", 0)])
        G.wait(wlast[("w", 1)])
        tok_sw = [None, None]
        def load_sw(i, ic):
            G.dma(swc[i][1][:].rearrange("p (dc t) -> p dc t", t=512),
                  sw1.rearrange("(dc p) o -> p dc o", p=128)[:, :, ic * 512:(ic + 1) * 512])
            G.dma(swc[i][2][:].rearrange("p (dc t) -> p dc t", t=512),
                  sw2.rearrange("(dc p) o -> p dc o", p=128)[:, :, ic * 512:(ic + 1) * 512])
            return G.dma(swc[i][3][:].rearrange("p (ic o) -> p ic o", o=1024),
                         sw3.rearrange("(icc p) o -> p icc o", p=128)[:, ic * 4:(ic + 1) * 4, :])
        tok_sw[0] = load_sw(0, 0)
        swlast = {}
        tza = None
        for ic in range(4):
            i = ic % 2
            if ic + 1 < 4:
                G.wait(swlast.get((ic + 1) % 2))
                tok_sw[(ic + 1) % 2] = load_sw((ic + 1) % 2, ic + 1)
            for isc in range(4):
                for tc in range(4):
                    b1k, b2k = isc % 2, 2 + isc % 2
                    T.wait(tok_sw[i])
                    T.wait(pmask.get(("sh", b1k)))
                    def h1mm(e, i=i, isc=isc, tc=tc, b1k=b1k):
                        for dc in range(8):
                            inst = e.matmul(
                                pb[b1k][:, :],
                                swc[i][1][:, dc * 512 + isc * 128:dc * 512 + (isc + 1) * 128],
                                xbf[:, dc * TB + tc * 512:dc * TB + (tc + 1) * 512],
                                start=(dc == 0), stop=(dc == 7))
                        return inst
                    th1 = T.op(h1mm, sig=True)
                    T.wait(pmask.get(("sh", b2k)))
                    def h2mm(e, i=i, isc=isc, tc=tc, b2k=b2k):
                        for dc in range(8):
                            inst = e.matmul(
                                pb[b2k][:, :],
                                swc[i][2][:, dc * 512 + isc * 128:dc * 512 + (isc + 1) * 128],
                                xbf[:, dc * TB + tc * 512:dc * TB + (tc + 1) * 512],
                                start=(dc == 0), stop=(dc == 7))
                        return inst
                    th2 = T.op(h2mm, sig=True)
                    A.wait(th1)
                    A.wait(swlast.get(("sl", isc % 2)))
                    tsl = A.op(lambda e, b1k=b1k, isc=isc: e.activation(
                        sl_sb[isc % 2][:, :], pb[b1k][:, :], AF.Silu), sig=True)
                    pmask[("sh", b1k)] = tsl
                    V.wait(tsl)
                    V.wait(th2)
                    V.wait(swlast.get("ztok"))
                    V.wait(wlast.get(("w", 1)))
                    thh = V.op(lambda e, b2k=b2k, isc=isc, tc=tc: e.tensor_mul(
                        hh_sb[:, isc * TB + tc * 512:isc * TB + (tc + 1) * 512],
                        pb[b2k][:, :], sl_sb[isc % 2][:, :]), sig=True)
                    pmask[("sh", b2k)] = thh
                    swlast[("sl", isc % 2)] = thh
                    swlast[("hh", isc, tc)] = thh
            for tc in range(4):
                for dc in range(8):
                    bz = 4 + dc % 2
                    T.wait(pmask.get(("sh", bz)))
                    for isc in range(4):
                        T.wait(swlast.get(("hh", isc, tc)))
                    def zmm(e, i=i, tc=tc, dc=dc, bz=bz):
                        for isc in range(4):
                            inst = e.matmul(
                                pb[bz][:, :],
                                swc[i][3][:, isc * 1024 + dc * 128:isc * 1024 + (dc + 1) * 128],
                                hh_sb[:, isc * TB + tc * 512:isc * TB + (tc + 1) * 512],
                                start=(isc == 0), stop=(isc == 3))
                        return inst
                    tz = T.op(zmm, sig=True)
                    V.wait(tz)
                    V.wait(tln)
                    tza = V.op(lambda e, dc=dc, tc=tc, bz=bz: e.tensor_add(
                        res_sb[:, dc * TB + tc * 512:dc * TB + (tc + 1) * 512],
                        res_sb[:, dc * TB + tc * 512:dc * TB + (tc + 1) * 512],
                        pb[bz][:, :]), sig=True)
                    pmask[("sh", bz)] = tza
            swlast[i] = tz
            swlast["ztok"] = tz
        tok_z = tza

        T.wait(tok_z)
        tok_ln4 = _emit_ln_stats_norm(p, nc, res_sb, xbf, sc,
                                      [pb[0], pb[1], pb[2], pb[3]], TB)
        SY.wait(tok_ln4)
        tout = SY.dma(outfm.rearrange("(dc p) t -> p dc t", p=128), resr)
        SY.wait(tout)
        p.flush()
    return nc



_BF = ml_dtypes.bfloat16


def _t(a):
    return np.ascontiguousarray(np.asarray(a).T.astype(np.float32))


_cache = {}


def _launch_A(inputs):
    if "A" not in _cache:
        _cache["A"] = build_launch_A()
    nc = _cache["A"]
    x = np.asarray(inputs["L0_attn_x"], np.float32)
    wq = np.asarray(inputs["L0_attn_Wq"], np.float32)
    wk = np.asarray(inputs["L0_attn_Wk"], np.float32)
    wv = np.asarray(inputs["L0_attn_Wv"], np.float32)
    wqT = _t(wq)
    wkT = _t(wk)
    wvT = _t(wv)
    in_maps = []
    for c in range(NCORES):
        b = c // 4
        hs = 4 * (c % 4)
        in_maps.append({
            "xbT": _t(x[b]),
            "wqT": np.ascontiguousarray(wqT[:, hs * DH:(hs + 4) * DH]),
            "wkT": wkT,
            "wvT": wvT,
        })
    res = run_bass_kernel_spmd(nc, in_maps, core_ids=list(range(NCORES)))
    attn_fm = np.zeros((B, D, S), np.float32)
    for c in range(NCORES):
        b = c // 4
        hs = 4 * (c % 4)
        attn_fm[b, hs * DH:(hs + 4) * DH, :] = res.results[c]["attn"]
    return attn_fm


def _launch_B(inputs, attn_fm):
    if "B" not in _cache:
        _cache["B"] = build_launch_B()
    nc = _cache["B"]
    x = np.asarray(inputs["L0_attn_x"], np.float32)
    w1 = np.asarray(inputs["L0_ffn_w1"], np.float32)
    w2 = np.asarray(inputs["L0_ffn_w2"], np.float32)
    w3 = np.asarray(inputs["L0_ffn_w3"], np.float32)
    w1T = np.ascontiguousarray(np.transpose(w1, (0, 2, 1)))
    w2T = np.ascontiguousarray(np.transpose(w2, (0, 2, 1)))
    w3T = np.ascontiguousarray(np.transpose(w3, (0, 2, 1)))
    sw1T = _t(inputs["L0_ffn_sw1"])
    sw2T = _t(inputs["L0_ffn_sw2"])
    sw3T = _t(inputs["L0_ffn_sw3"])
    gate = np.asarray(inputs["L0_ffn_gate_w"], np.float32)
    gate_pk = np.ascontiguousarray(
        gate.reshape(8, 8, 128).transpose(2, 1, 0).reshape(128, 64))
    bias_pk = np.zeros((128, 192), np.float32)
    for j, nm in enumerate(("b1", "b2", "b3")):
        bj = np.asarray(inputs[f"L0_ffn_{nm}"], np.float32).reshape(8, 8, 128)
        bias_pk[:, :] += 0.0
        for e in range(8):
            for c in range(8):
                bias_pk[:, (e * 3 + j) * 8 + c] = bj[e, c, :]
    ident = np.zeros((128, 512), np.float32)
    ident[:, 0:128] = np.eye(128, dtype=np.float32)
    oneh = np.zeros((40, 8 * 128), np.float32)
    for e in range(8):
        oneh[e, e * 128:(e + 1) * 128] = 1.0
        oneh[32 + e, e * 128:(e + 1) * 128] = 1.0
    xT = [None] * NCORES
    in_maps = []
    for c in range(NCORES):
        b = c // 4
        t0 = (c % 4) * TB
        in_maps.append({
            "xT": np.ascontiguousarray(_t(x[b])[:, t0:t0 + TB]),
            "attnT": np.ascontiguousarray(attn_fm[b][:, t0:t0 + TB]),
            "w1T": w1T, "w2T": w2T, "w3T": w3T,
            "sw1T": sw1T, "sw2T": sw2T, "sw3T": sw3T,
            "gate_pk": gate_pk, "bias_pk": bias_pk,
            "ident_in": ident, "oneh_in": oneh,
        })
    res = run_bass_kernel_spmd(nc, in_maps, core_ids=list(range(NCORES)))
    x2_fm = np.zeros((B, D, S), np.float32)
    for c in range(NCORES):
        b = c // 4
        t0 = (c % 4) * TB
        x2_fm[b][:, t0:t0 + TB] = res.results[c]["x2out"]
    return x2_fm


def _rope_tables():
    inv = 1.0 / (10000.0 ** (np.arange(0, 64, 2, dtype=np.float32) / 64.0))
    t = np.arange(S, dtype=np.float32)
    fr = np.outer(t, inv).astype(np.float32)
    emb = np.concatenate([fr, fr], axis=1)
    cos = np.cos(emb).astype(np.float32)
    sin = np.sin(emb).astype(np.float32)
    c2 = np.zeros((128, S), np.float32)
    s2 = np.zeros((128, S), np.float32)
    for p in range(128):
        c2[p] = cos[:, p % 64]
        s2[p] = sin[:, p % 64]
    return c2, s2


def _launch_C(inputs, x2_fm):
    if "C" not in _cache:
        _cache["C"] = build_launch_C()
    nc = _cache["C"]
    wq = np.asarray(inputs["L1_attn_Wq"], np.float32)
    wk = np.asarray(inputs["L1_attn_Wk"], np.float32)
    wv = np.asarray(inputs["L1_attn_Wv"], np.float32)
    c2, s2 = _rope_tables()
    R = np.zeros((64, 64), np.float32)
    for i in range(32):
        R[i, i + 32] = -1.0
    for i in range(32, 64):
        R[i, i - 32] = 1.0
    RT = np.ascontiguousarray(R.T)
    r128 = np.zeros((128, 128), np.float32)
    r128[0:64, 0:64] = RT
    r128[64:128, 64:128] = RT
    maskc = np.zeros((128, 4 * 512), np.float32)
    for m in range(4):
        for i_ in range(128):
            jj = np.arange(512)
            maskc[i_, m * 512:(m + 1) * 512] = (i_ <= jj - 128 * m).astype(np.float32)
    in_maps = []
    for c in range(NCORES):
        b = c // 4
        hs = 4 * (c % 4)
        in_maps.append({
            "x2T": np.ascontiguousarray(x2_fm[b]),
            "wqTh": np.ascontiguousarray(_t(wq)[:, hs * 64:(hs + 4) * 64]),
            "wkTh": np.ascontiguousarray(_t(wk)[:, hs * 64:(hs + 4) * 64]),
            "wvTh": np.ascontiguousarray(_t(wv)[:, hs * 64:(hs + 4) * 64]),
            "cos2": c2, "sin2": s2, "r128T": r128, "maskc": maskc,
        })
    res = run_bass_kernel_spmd(nc, in_maps, core_ids=list(range(NCORES)))
    attn1_fm = np.zeros((B, D, S), np.float32)
    for c in range(NCORES):
        b = c // 4
        hs = 4 * (c % 4)
        attn1_fm[b, hs * 64:(hs + 4) * 64, :] = res.results[c]["attn1"]
    return attn1_fm


def _launch_D(inputs, x2_fm, attn1_fm):
    if "Dl" not in _cache:
        _cache["Dl"] = build_launch_D()
    nc = _cache["Dl"]
    wo = np.asarray(inputs["L1_attn_Wo"], np.float32)
    woT = _t(wo)
    sw1 = np.ascontiguousarray(np.asarray(inputs["L1_ffn_sw1"], np.float32))
    sw2 = np.ascontiguousarray(np.asarray(inputs["L1_ffn_sw2"], np.float32))
    sw3 = np.ascontiguousarray(np.asarray(inputs["L1_ffn_sw3"], np.float32))
    ew1 = np.ascontiguousarray(np.asarray(inputs["L1_ffn_ew1"], np.float32))
    ew2 = np.ascontiguousarray(np.asarray(inputs["L1_ffn_ew2"], np.float32))
    ew3 = np.ascontiguousarray(np.asarray(inputs["L1_ffn_ew3"], np.float32))
    rw = np.asarray(inputs["L1_ffn_rw"], np.float32)
    rb = np.asarray(inputs["L1_ffn_rb"], np.float32).reshape(1, 8)
    rw_pk = np.ascontiguousarray(
        rw.reshape(8, 128, 8).transpose(1, 0, 2).reshape(128, 64))
    ident = np.zeros((128, 512), np.float32)
    ident[:, 0:128] = np.eye(128, dtype=np.float32)
    oneh = np.zeros((40, 8 * 128), np.float32)
    for e in range(8):
        oneh[e, e * 128:(e + 1) * 128] = 1.0
        oneh[32 + e, e * 128:(e + 1) * 128] = 1.0
    in_maps = []
    for c in range(NCORES):
        b = c // 4
        t0 = (c % 4) * TB
        in_maps.append({
            "x2Ts": np.ascontiguousarray(x2_fm[b][:, t0:t0 + TB]),
            "attn1Ts": np.ascontiguousarray(attn1_fm[b][:, t0:t0 + TB]),
            "woT": woT, "sw1": sw1, "sw2": sw2, "sw3": sw3,
            "ew1": ew1, "ew2": ew2, "ew3": ew3,
            "rw_pk": rw_pk, "rb_in": rb,
            "ident_in": ident, "oneh_in": oneh,
        })
    res = run_bass_kernel_spmd(nc, in_maps, core_ids=list(range(NCORES)))
    out = np.zeros((B, S, D), np.float32)
    for c in range(NCORES):
        b = c // 4
        t0 = (c % 4) * TB
        out[b, t0:t0 + TB, :] = res.results[c]["outfm"].T
    return out


def kernel(**inputs):
    attn_fm = _launch_A(inputs)
    x2_fm = _launch_B(inputs, attn_fm)
    attn1_fm = _launch_C(inputs, x2_fm)
    out = _launch_D(inputs, x2_fm, attn1_fm)
    return out
